# version 06
# baseline (speedup 1.0000x reference)
import sys
sys.path.insert(0, '/opt/trn_rl_repo')
import contextlib
import numpy as np
import ml_dtypes
import concourse.bass as bass
import concourse.mybir as mybir
from concourse.bass_utils import run_bass_kernel_spmd

F32 = mybir.dt.float32
BF16 = mybir.dt.bfloat16
AF = mybir.ActivationFunctionType
ALU = mybir.AluOpType
AX = mybir.AxisListType

B, S, D = 4, 4096, 2048
NIC = D // 128
NEG_SHIFT = -12.0
EPS = 1e-5
ENGS = ("sync", "tensor", "vector", "scalar", "gpsimd")


class Sem:
    def __init__(self, handle):
        self.h = handle
        self.v = 0

    def mark(self):
        return (self, self.v)


class Phase:

    def __init__(self):
        self.q = {e: [] for e in ENGS}

    def do(self, eng, fn, wait=(), inc=None, inck=1):
        waits = [w for w in wait if w is not None]
        if inc is not None:
            inc.v += inck
        self.q[eng].append((fn, waits, inc, inck))

    def run(self, nc, name):
        with nc.Block(name) as blk:
            def register(eng_name):
                def body(eng):
                    for fn, waits, inc, inck in self.q[eng_name]:
                        for s, val in waits:
                            if val > 0:
                                eng.wait_ge(s.h, val)
                        ins = fn(eng)
                        if inc is not None:
                            ins.then_inc(inc.h, inck)
                return body
            for e in ENGS:
                if self.q[e]:
                    getattr(blk, e)(register(e))


def pair_heads(p):
    h1 = 8 * (p // 4) + (p % 4)
    return h1, h1 + 4


def build_program():
    nc = bass.Bass("TRN2", target_bir_lowering=False)

    xfull = nc.dram_tensor("xfull", [S, D], F32, kind="ExternalInput").ap()
    xq = nc.dram_tensor("xq", [2048, D], F32, kind="ExternalInput").ap()
    wq_in = nc.dram_tensor("wq", [D, D], F32, kind="ExternalInput").ap()
    wk_in = nc.dram_tensor("wk", [D, 512], F32, kind="ExternalInput").ap()
    wv_in = nc.dram_tensor("wv", [D, 512], F32, kind="ExternalInput").ap()
    wo_in = nc.dram_tensor("wo", [D, D], F32, kind="ExternalInput").ap()
    wf_in = nc.dram_tensor("wffn", [D, D], F32, kind="ExternalInput").ap()
    sc_in = nc.dram_tensor("scale", [D], F32, kind="ExternalInput")
    bi_in = nc.dram_tensor("bias", [D], F32, kind="ExternalInput").ap()
    mk_in = nc.dram_tensor("masks", [8, 128, 512], BF16, kind="ExternalInput").ap()
    id_in = nc.dram_tensor("ident", [128, 128], BF16, kind="ExternalInput").ap()
    out_d = nc.dram_tensor("out", [2048, D], F32, kind="ExternalOutput").ap()

    wqbf_h = nc.dram_tensor("wq_bf", [16, 128, NIC * 128], BF16, kind="Internal")
    wobf = nc.dram_tensor("wo_bf", [D, D], BF16, kind="Internal").ap()
    wfbf_h = nc.dram_tensor("wfT_bf", [D, D], BF16, kind="Internal")
    rb_h = nc.dram_tensor("rbounce", [2, 2048], F32, kind="Internal")
    wqbf = wqbf_h.ap()
    wfbf = wfbf_h.ap()

    gst = contextlib.ExitStack()
    with gst:
        ident = gst.enter_context(nc.sbuf_tensor("identsb", [128, 128], BF16))
        masksb = gst.enter_context(nc.sbuf_tensor("masksb", [128, 4096], BF16))
        onesrow = gst.enter_context(nc.sbuf_tensor("onesrow", [1, 128], BF16))
        negMc = gst.enter_context(nc.sbuf_tensor("negMc", [128, 1], F32))
        epsc = gst.enter_context(nc.sbuf_tensor("epsc", [128, 1], F32))
        browb = gst.enter_context(nc.sbuf_tensor("browb", [1, D], BF16))
        scol = gst.enter_context(nc.sbuf_tensor("scol", [128, 16], F32))
        Kf = [gst.enter_context(nc.sbuf_tensor(f"Kf{c}", [128, S], BF16))
              for c in range(4)]
        Vt = [gst.enter_context(nc.sbuf_tensor(f"Vt{g}", [128, 520], BF16))
              for g in range(32)]
        stats = [gst.enter_context(nc.sbuf_tensor(f"stats{t}", [128, 12], F32))
                 for t in range(4)]
        XH = [gst.enter_context(nc.sbuf_tensor(f"XHb{i}", [128, D], BF16))
              for i in range(2)]
        XSQ = gst.enter_context(nc.sbuf_tensor("XSQ", [128, D], BF16))

        sm = {n: Sem(gst.enter_context(nc.semaphore(n)))
              for n in ("sX", "sW", "sSC", "sRB", "sOUT", "sPE", "sS", "sPV",
                        "sA", "sE", "sV", "sV2", "sVe", "sVr", "sVo", "sM")}
        sX, sW, sSC, sRB, sOUT = (sm[k] for k in ("sX", "sW", "sSC", "sRB", "sOUT"))
        sPE, sS, sPV = sm["sPE"], sm["sS"], sm["sPV"]
        sA, sE = sm["sA"], sm["sE"]
        sV, sV2, sVe, sVr, sVo = (sm[k] for k in ("sV", "sV2", "sVe", "sVr", "sVo"))
        sM = sm["sM"]

        def ln_front(P, stt, src, src_wait):
            P.do("scalar", lambda a: a.activation(
                out=XSQ[:], in_=src, func=AF.Square, accum_out=stt[:, 1:2]),
                wait=src_wait, inc=sA)
            sq_mk = sA.mark()
            P.do("vector", lambda v: v.reduce_sum(out=stt[:, 0:1], in_=src,
                                                  axis=AX.X), wait=src_wait,
                inc=sV2)
            rs_mk = sV2.mark()
            P.do("gpsimd", lambda g: g.tensor_scalar(
                out=stt[:, 2:3], in0=stt[:, 0:1], scalar1=1.0 / D, scalar2=None,
                op0=ALU.mult), wait=[rs_mk], inc=sM)
            m1 = sM.mark()
            P.do("gpsimd", lambda g: g.tensor_scalar(
                out=stt[:, 3:4], in0=stt[:, 1:2], scalar1=1.0 / D, scalar2=None,
                op0=ALU.mult), wait=[sq_mk])
            P.do("gpsimd", lambda g: g.tensor_mul(
                out=stt[:, 6:7], in0=stt[:, 2:3], in1=stt[:, 2:3]),
                wait=[m1], inc=sM)
            m2 = sM.mark()
            P.do("gpsimd", lambda g: g.tensor_sub(
                out=stt[:, 3:4], in0=stt[:, 3:4], in1=stt[:, 6:7]),
                wait=[m2], inc=sM)
            var_mk = sM.mark()
            P.do("gpsimd", lambda g: g.tensor_scalar(
                out=stt[:, 5:6], in0=stt[:, 2:3], scalar1=-1.0, scalar2=None,
                op0=ALU.mult), inc=sM)
            negm_mk = sM.mark()
            P.do("scalar", lambda a: a.activation(
                out=stt[:, 6:7], in_=stt[:, 3:4], func=AF.Ln, bias=epsc[:]),
                wait=[var_mk], inc=sA)
            wa = sA.mark()
            P.do("scalar", lambda a: a.activation(
                out=stt[:, 4:5], in_=stt[:, 6:7], func=AF.Exp, scale=-0.5),
                wait=[wa], inc=sA)
            rstd_mk = sA.mark()
            return (negm_mk, rstd_mk)

        def ln_back(P, stt, src, dst, fm, dst_free=None):
            negm_mk, rstd_mk = fm
            P.do("vector", lambda v: v.tensor_scalar(
                out=dst, in0=src, scalar1=stt[:, 5:6], scalar2=stt[:, 4:5],
                op0=ALU.add, op1=ALU.mult),
                wait=[negm_mk, rstd_mk, dst_free], inc=sV)
            return sV.mark()

        def transpose_tile(P, srcT, t, dstbuf, psTb, psT_free, ready_mark,
                           extra_wait=None):
            last_pe = None
            for q in range(4):
                qi = (4 * t + q) % 2
                for u in range(4):
                    icc = 4 * q + u
                    P.do("tensor", lambda e, qi=qi, u=u, icc=icc:
                         e.matmul(psTb[qi][:, 128 * u:128 * u + 128],
                                  srcT[:, 128 * icc:128 * icc + 128],
                                  ident[:], start=True, stop=True),
                         wait=([ready_mark, psT_free[qi], extra_wait]
                               if u == 0 else []),
                         inc=(sPE if u == 3 else None))
                pe_mk = sPE.mark()
                dst = bass.AP(dstbuf, 2048 * q + 128 * t,
                              [[16 * 512, 128], [512, 4], [1, 128]])
                src3 = bass.AP(psTb[qi], 0, [[512, 128], [128, 4], [1, 128]])
                P.do("vector", lambda e, dst=dst, src3=src3: e.tensor_copy(
                    out=dst, in_=src3), wait=[pe_mk], inc=sVe)
                psT_free[qi] = sVe.mark()
                last_pe = pe_mk
            return sVe.mark(), last_pe

        P = Phase()
        P.do("sync", lambda e: e.dma_start(out=ident[:], in_=id_in),
             inc=sX, inck=16)
        for t in range(8):
            P.do("sync", lambda e, t=t: e.dma_start(
                out=masksb[:, 512 * t:512 * t + 512], in_=mk_in[t]),
                inc=sX, inck=16)
        for c in range(16):
            P.do("sync", lambda e, c=c: e.dma_start(
                out=scol[:, c:c + 1],
                in_=bass.AP(sc_in, 128 * c, [[1, 128], [1, 1]])),
                inc=sX, inck=16)
        P.do("vector", lambda e: e.memset(onesrow[:], 1.0))
        P.do("vector", lambda e: e.memset(negMc[:], NEG_SHIFT))
        P.do("vector", lambda e: e.memset(epsc[:], EPS))
        for g in range(32):
            P.do("gpsimd", lambda e, g=g: e.memset(Vt[g][:], 1.0))
        P.do("gpsimd", lambda e: e.engine_nop(), wait=[sX.mark()])
        P.run(nc, "init")

        with contextlib.ExitStack() as pst:
            WFST = [pst.enter_context(nc.sbuf_tensor(f"WFSp{i}", [128, D], F32))
                    for i in range(2)]
            BST = [pst.enter_context(nc.sbuf_tensor(f"BSTp{i}", [128, D], BF16))
                   for i in range(3)]
            FST = [pst.enter_context(nc.sbuf_tensor(f"FSTp{i}", [128, 512], BF16))
                   for i in range(2)]
            dsc = pst.enter_context(nc.sbuf_tensor("dsc", [128, D], BF16))
            brst = pst.enter_context(nc.sbuf_tensor("brst", [1, D], F32))
            psT = [pst.enter_context(nc.psum_tensor(f"psTp{i}", [128, 512], F32))
                   for i in range(2)]
            WFS1 = [pst.enter_context(nc.sbuf_tensor(f"WFS1{i}", [128, 512], F32))
                    for i in range(2)]
            WKB = pst.enter_context(nc.sbuf_tensor("WKB", [128, 16 * 512], BF16))
            WVB = pst.enter_context(nc.sbuf_tensor("WVB", [128, 16 * 512], BF16))
            xt = [pst.enter_context(nc.sbuf_tensor(f"xt{i}", [128, D], F32))
                  for i in range(2)]
            PXF = pst.enter_context(nc.sbuf_tensor("PXF", [128, 16 * 512], BF16))
            psT1 = [pst.enter_context(nc.psum_tensor(f"psT1{i}", [128, 512], F32))
                    for i in range(2)]
            pkv = [pst.enter_context(nc.psum_tensor(f"pkv{i}", [128, 512], F32))
                   for i in range(2)]

            P = Phase()
            P.do("sync", lambda e: e.dma_start(out=brst[:], in_=bi_in),
                 inc=sW, inck=16)
            P.do("vector", lambda e: e.tensor_copy(out=browb[:], in_=brst[:]),
                 wait=[sW.mark()], inc=sM)
            for c in range(16):
                P.do("gpsimd", lambda e, c=c: e.tensor_scalar(
                    out=dsc[:, 128 * c:128 * c + 128], in0=ident[:],
                    scalar1=scol[:, c:c + 1], scalar2=None, op0=ALU.mult),
                    inc=(sM if c == 15 else None))
            dsc_done = sM.mark()
            wfs_free = [None, None]
            for sl in range(32):
                i = sl % 2
                src = wk_in if sl < 16 else wv_in
                dstT = WKB if sl < 16 else WVB
                ic = sl % 16
                P.do("sync", lambda e, src=src, ic=ic, i=i: e.dma_start(
                    out=WFS1[i][:], in_=src[128 * ic:128 * ic + 128, :]),
                    wait=[wfs_free[i]], inc=sW, inck=16)
                P.do("vector", lambda e, dstT=dstT, ic=ic, i=i: e.tensor_copy(
                    out=dstT[:, 512 * ic:512 * ic + 512], in_=WFS1[i][:]),
                    wait=[sW.mark()], inc=sM)
                wfs_free[i] = sM.mark()
            wkv_cast = sM.mark()

            wfst_free = [None, None]
            bst_free = [None, None, None]
            psTp_free = [None, None]
            fst_free = [None, None]
            xt_free = [None, None]
            psT_free = [None, None]
            pkv_free = [None, None]
            xh_free = [None, None]
            pxf_free = None
            pend = None

            def emit_prep_slab(sl):
                kind = "wq" if sl < 16 else ("wo" if sl < 32 else "wf")
                ic = sl % 16
                src = {"wq": wq_in, "wo": wo_in, "wf": wf_in}[kind]
                i = sl % 2
                ib = sl % 3
                P.do("sync", lambda e, src=src, ic=ic, i=i: e.dma_start(
                    out=WFST[i][:], in_=src[128 * ic:128 * ic + 128, :]),
                    wait=[wfst_free[i]], inc=sW, inck=16)
                P.do("vector", lambda e, i=i, ib=ib: e.tensor_copy(
                    out=BST[ib][:], in_=WFST[i][:]),
                    wait=[sW.mark(), bst_free[ib]], inc=sM)
                wfst_free[i] = sM.mark()
                cast_mk = sM.mark()
                if kind == "wq":
                    for p in range(16):
                        h1, _ = pair_heads(p)
                        dst = bass.AP(wqbf_h, (p * 128) * (NIC * 128) + ic * 128,
                                      [[NIC * 128, 128], [64, 2], [1, 64]])
                        eng = "scalar" if p % 2 == 0 else "sync"
                        P.do(eng, lambda e, dst=dst, ib=ib, h1=h1: e.dma_start(
                            out=dst,
                            in_=bass.AP(BST[ib], 64 * h1,
                                        [[D, 128], [256, 2], [1, 64]])),
                            wait=[cast_mk], inc=sSC, inck=16)
                    bst_free[ib] = sSC.mark()
                elif kind == "wo":
                    P.do("scalar", lambda e, ic=ic, ib=ib: e.dma_start(
                        out=wobf[128 * ic:128 * ic + 128, :], in_=BST[ib][:]),
                        wait=[cast_mk], inc=sSC, inck=16)
                    bst_free[ib] = sSC.mark()
                else:
                    oc = ic
                    for q in range(4):
                        qi = q % 2
                        for u in range(4):
                            icc = 4 * q + u
                            P.do("tensor", lambda e, ib=ib, qi=qi, u=u, icc=icc,
                                 oc=oc: e.matmul(
                                     psT[qi][:, 128 * u:128 * u + 128],
                                     BST[ib][:, 128 * icc:128 * icc + 128],
                                     dsc[:, 128 * oc:128 * oc + 128],
                                     start=True, stop=True),
                                 wait=([cast_mk, dsc_done, psTp_free[qi]]
                                       if u == 0 else []),
                                 inc=(sPE if u == 3 else None))
                        pe_mk = sPE.mark()
                        P.do("vector", lambda e, qi=qi: e.tensor_copy(
                            out=FST[qi][:], in_=psT[qi][:]),
                            wait=[pe_mk, fst_free[qi]], inc=sV)
                        psTp_free[qi] = sV.mark()
                        dstw = bass.AP(wfbf_h, (512 * q) * D + 128 * oc,
                                       [[D, 128], [128 * D, 4], [1, 128]])
                        P.do("scalar", lambda e, qi=qi, dstw=dstw: e.dma_start(
                            out=dstw, in_=FST[qi][:]),
                            wait=[sV.mark()], inc=sSC, inck=16)
                        fst_free[qi] = sSC.mark()
                    bst_free[ib] = sPE.mark()

            def emit_kv_tile(kt):
                nonlocal pend, pxf_free
                if kt < 32:
                    i = kt % 2
                    P.do("sync", lambda e, kt=kt, i=i: e.dma_start(
                        out=xt[i][:], in_=xfull[128 * kt:128 * kt + 128, :]),
                        wait=[xt_free[i]], inc=sX, inck=16)
                    xmk = sX.mark()
                    fm = ln_front(P, stats[kt % 4], xt[i][:], [xmk])
                    nxt = (kt, i, kt % 2, fm)
                else:
                    nxt = None
                if pend is not None:
                    kt0, i0, ih0, fm0 = pend
                    sg, v = kt0 // 4, kt0 % 4
                    xh_mk = ln_back(P, stats[kt0 % 4], xt[i0][:], XH[ih0][:],
                                    fm0, dst_free=xh_free[ih0])
                    xt_free[i0] = xh_mk
                    _, last_pe = transpose_tile(P, XH[ih0], v, PXF, psT1,
                                                psT_free, xh_mk,
                                                extra_wait=pxf_free)
                    xh_free[ih0] = last_pe
                    if v == 3:
                        xf_ready = sVe.mark()
                        for g in range(8):
                            gi = g % 2
                            if g < 4:
                                kfc = g
                                for icx in range(NIC):
                                    P.do("tensor", lambda e, gi=gi, icx=icx,
                                         kfc=kfc: e.matmul(
                                             pkv[gi][:],
                                             WKB[:, 512 * icx + 128 * kfc:
                                                 512 * icx + 128 * kfc + 128],
                                             PXF[:, 512 * icx:512 * icx + 512],
                                             start=(icx == 0), stop=(icx == 15)),
                                         wait=([xf_ready, wkv_cast, pkv_free[gi]]
                                               if icx == 0 else []),
                                         inc=(sPE if icx == 15 else None))
                                pe_mk = sPE.mark()
                                P.do("vector", lambda e, gi=gi, kfc=kfc, sg=sg:
                                     e.tensor_copy(
                                         out=Kf[kfc][:, 512 * sg:512 * sg + 512],
                                         in_=pkv[gi][:]), wait=[pe_mk], inc=sVe)
                            else:
                                vtile = g - 4
                                for icx in range(NIC):
                                    P.do("tensor", lambda e, gi=gi, icx=icx,
                                         vt2=vtile: e.matmul(
                                             pkv[gi][:],
                                             PXF[:, 512 * icx + 128 * vt2:
                                                 512 * icx + 128 * vt2 + 128],
                                             WVB[:, 512 * icx:512 * icx + 512],
                                             start=(icx == 0), stop=(icx == 15)),
                                         wait=([pkv_free[gi]] if icx == 0 else []),
                                         inc=(sPE if icx == 15 else None))
                                pe_mk = sPE.mark()
                                dstv = bass.AP(Vt[4 * sg + vtile], 0,
                                               [[520, 128], [65, 8], [1, 64]])
                                srcp = bass.AP(pkv[gi], 0,
                                               [[512, 128], [64, 8], [1, 64]])
                                P.do("vector", lambda e, dstv=dstv, srcp=srcp:
                                     e.tensor_copy(out=dstv, in_=srcp),
                                     wait=[pe_mk], inc=sVe)
                            pkv_free[gi] = sVe.mark()
                        pxf_free = sPE.mark()
                pend = nxt

            for step in range(48):
                emit_prep_slab(step)
                if step < 33:
                    emit_kv_tile(step)
            prep_stores = sSC.mark()
            P.run(nc, "prepkv")


        with contextlib.ExitStack() as pst:
            XG = [pst.enter_context(nc.sbuf_tensor(f"XG{t}", [128, D], F32))
                  for t in range(4)]
            XF = pst.enter_context(nc.sbuf_tensor("XF", [128, 16 * 512], BF16))
            QP = pst.enter_context(nc.sbuf_tensor("QP", [128, 16 * 512], BF16))
            OF = pst.enter_context(nc.sbuf_tensor("OF", [128, 16 * 512], BF16))
            EB = [pst.enter_context(nc.sbuf_tensor(f"EB{i}", [128, 1024], BF16))
                  for i in range(2)]
            WST = [pst.enter_context(nc.sbuf_tensor(f"WST{i}", [128, D], BF16))
                   for i in range(4)]
            OSTG = [pst.enter_context(nc.sbuf_tensor(f"OSTG{i}", [65, 1024], F32))
                    for i in range(2)]
            RBa = [pst.enter_context(nc.sbuf_tensor(f"RBa{i}", [64, 512], F32))
                   for i in range(2)]
            RBb = [pst.enter_context(nc.sbuf_tensor(f"RBb{i}", [64, 512], F32))
                   for i in range(2)]
            RR1 = [pst.enter_context(nc.sbuf_tensor(f"RR1{i}", [64, 16], F32))
                   for i in range(2)]
            RR2 = [pst.enter_context(nc.sbuf_tensor(f"RR2{i}", [64, 16], F32))
                   for i in range(2)]

            xg_store = [None] * 4
            for j in range(4):
                NK = 8 * (j + 1)

                with contextlib.ExitStack() as ast:
                    psTa = [ast.enter_context(
                        nc.psum_tensor(f"psTa{j}{i}", [128, 512], F32))
                        for i in range(2)]
                    psQ = [ast.enter_context(
                        nc.psum_tensor(f"psQ{j}{i}", [128, 512], F32))
                        for i in range(2)]
                    P = Phase()
                    psT_free = [None, None]
                    psQ_free = [None, None]
                    xh_free = [None, None]
                    wst_free = [None] * 4
                    pend = None
                    for t in range(5):
                        if t < 4:
                            P.do("sync", lambda e, t=t, j=j: e.dma_start(
                                out=XG[t][:],
                                in_=xq[512 * j + 128 * t:512 * j + 128 * t + 128, :]),
                                wait=[xg_store[t]], inc=sX, inck=16)
                            xmk = sX.mark()
                            fm = ln_front(P, stats[t], XG[t][:], [xmk])
                            nxt = (t, fm)
                        else:
                            nxt = None
                        if pend is not None:
                            t0, fm0 = pend
                            xh_mk = ln_back(P, stats[t0], XG[t0][:], XH[t0 % 2][:],
                                            fm0, dst_free=xh_free[t0 % 2])
                            _, last_pe = transpose_tile(P, XH[t0 % 2], t0, XF,
                                                        psTa, psT_free, xh_mk)
                            xh_free[t0 % 2] = last_pe
                        pend = nxt
                    xf_ready = sVe.mark()
                    for p in range(16):
                        wi = p % 4
                        P.do("sync", lambda e, p=p, wi=wi: e.dma_start(
                            out=WST[wi][:], in_=wqbf[p]),
                            wait=[wst_free[wi],
                                  (prep_stores if (j == 0 and p == 0) else None)],
                            inc=sW, inck=16)
                        wmk = sW.mark()
                        pi = p % 2
                        for icx in range(NIC):
                            P.do("tensor", lambda e, pi=pi, wi=wi, icx=icx:
                                 e.matmul(psQ[pi][:],
                                          WST[wi][:, 128 * icx:128 * icx + 128],
                                          XF[:, 512 * icx:512 * icx + 512],
                                          start=(icx == 0), stop=(icx == 15)),
                                 wait=([wmk, xf_ready, psQ_free[pi]]
                                       if icx == 0 else []),
                                 inc=(sPE if icx == 15 else None))
                        wst_free[wi] = sPE.mark()
                        pe_mk = sPE.mark()
                        P.do("vector", lambda e, pi=pi, p=p: e.tensor_copy(
                            out=QP[:, 512 * p:512 * p + 512], in_=psQ[pi][:]),
                            wait=[pe_mk], inc=sVe)
                        psQ_free[pi] = sVe.mark()
                    P.run(nc, f"ga{j}")

                with contextlib.ExitStack() as ast:
                    psS = [ast.enter_context(
                        nc.psum_tensor(f"psS{j}{i}", [128, 1024], F32))
                        for i in range(2)]
                    psO = [ast.enter_context(
                        nc.psum_tensor(f"psO{j}{i}", [128, 512], F32))
                        for i in range(4)]
                    P = Phase()
                    expv, maskv, pvv, psofree = {}, {}, {}, {}
                    evacv, outb, outb2, inb = {}, {}, {}, {}

                    def sched_pv(i2, p2, k2):
                        h1, h2 = pair_heads(p2)
                        kv1, kv2 = h1 // 4, h2 // 4
                        wl = []
                        if k2 == 0 and p2 >= 2:
                            wl.append(psofree[p2 - 2])
                        wl.append(maskv[i2] if k2 >= NK - 8 else expv[i2])
                        bi = i2 % 2
                        P.do("tensor", lambda e, p2=p2, k2=k2, kv1=kv1, bi=bi:
                             e.matmul(psO[p2 % 2][0:65, :],
                                      Vt[k2][:, 65 * kv1:65 * kv1 + 65],
                                      EB[bi][:, 0:512],
                                      start=(k2 == 0), stop=(k2 == NK - 1)),
                             wait=wl)
                        P.do("tensor", lambda e, p2=p2, k2=k2, kv2=kv2, bi=bi:
                             e.matmul(psO[2 + p2 % 2][0:65, :],
                                      Vt[k2][:, 65 * kv2:65 * kv2 + 65],
                                      EB[bi][:, 512:1024],
                                      start=(k2 == 0), stop=(k2 == NK - 1)),
                             inc=sPV)
                        pvv[i2] = sPV.mark()

                    pending_dve = []

                    def sched_epilogue(p2):
                        h1, h2 = pair_heads(p2)
                        pp = p2 % 2
                        base = pp * 2048
                        wl = [pvv[p2 * NK + NK - 1]]
                        if p2 >= 1:
                            wl.append(outb[p2 - 1])

                        def cl_copy(wl=wl, pp=pp, p2=p2):
                            wl2 = list(wl)
                            if p2 >= 2:
                                wl2.append(evacv[p2 - 2])
                            P.do("vector", lambda e: e.tensor_copy(
                                out=OSTG[pp][:, 0:512], in_=psO[pp][0:65, :]),
                                wait=wl2)
                            P.do("vector", lambda e: e.tensor_copy(
                                out=OSTG[pp][:, 512:1024],
                                in_=psO[2 + pp][0:65, :]), inc=sVr)
                            psofree[p2] = sVr.mark()
                        pending_dve.append(cl_copy)
                        svr1 = (sVr, sVr.v + 1)
                        sVr.v += 0
                        P.do("sync", lambda e, base=base, pp=pp: e.dma_start(
                            out=bass.AP(rb_h, base, [[2048, 1], [1, 1024]]),
                            in_=OSTG[pp][64:65, :]),
                            wait=[svr1], inc=sRB, inck=16)
                        outb[p2] = sRB.mark()
                        P.do("sync", lambda e, pp=pp, base=base: e.dma_start(
                            out=RR1[pp][:],
                            in_=bass.AP(rb_h, base, [[16, 64], [1, 16]])),
                            wait=[outb[p2]], inc=sRB, inck=16)
                        rr1_mk = sRB.mark()

                        def cl_recip(rr1_mk=rr1_mk, pp=pp, p2=p2):
                            P.do("vector", lambda e: e.reciprocal(
                                out=RR2[pp][:], in_=RR1[pp][:]),
                                wait=[rr1_mk,
                                      (outb2.get(p2 - 2) if p2 >= 2 else None)],
                                inc=sVr)
                        pending_dve.append(cl_recip)
                        svr2 = (sVr, sVr.v + 2)
                        P.do("sync", lambda e, pp=pp, base=base: e.dma_start(
                            out=bass.AP(rb_h, base + 1024, [[16, 64], [1, 16]]),
                            in_=RR2[pp][:]),
                            wait=[svr2], inc=sRB, inck=16)
                        outb2[p2] = sRB.mark()
                        P.do("sync", lambda e, pp=pp, base=base: e.dma_start(
                            out=RBa[pp][:],
                            in_=bass.AP(rb_h, base + 1024, [[0, 64], [1, 512]])),
                            wait=[outb2[p2],
                                  (evacv[p2 - 2] if p2 >= 2 else None)],
                            inc=sRB, inck=16)
                        P.do("sync", lambda e, pp=pp, base=base: e.dma_start(
                            out=RBb[pp][:],
                            in_=bass.AP(rb_h, base + 1024 + 512,
                                        [[0, 64], [1, 512]])),
                            inc=sRB, inck=16)
                        inb[p2] = sRB.mark()
                        inb_mk = inb[p2]

                        def cl_mult(inb_mk=inb_mk, pp=pp, h1=h1, h2=h2, p2=p2):
                            for hh, col, rbuf in ((h1, 0, RBa[pp]),
                                                  (h2, 512, RBb[pp])):
                                dst = OF[64 * (hh % 2):64 * (hh % 2) + 64,
                                         512 * (hh // 2):512 * (hh // 2) + 512]
                                P.do("vector",
                                     lambda e, col=col, rbuf=rbuf, dst=dst,
                                     pp=pp: e.tensor_tensor(
                                         out=dst,
                                         in0=OSTG[pp][0:64, col:col + 512],
                                         in1=rbuf[:], op=ALU.mult),
                                     wait=[inb_mk],
                                     inc=(sVo if hh == h2 else None))
                            evacv[p2] = sVo.mark()
                        pending_dve.append(cl_mult)

                    iters = [(p, k) for p in range(16) for k in range(NK)]
                    for i, (p, k) in enumerate(iters):
                        c1 = p // 4
                        bi = i % 2
                        wl = [expv[i - 2]] if i >= 2 else []
                        P.do("tensor", lambda e, c1=c1, k=k, p=p, bi=bi:
                             e.matmul(psS[bi][:, 0:512],
                                      Kf[c1][0:64, 128 * k:128 * k + 128],
                                      QP[0:64, 512 * p:512 * p + 512],
                                      start=True, stop=True), wait=wl)
                        P.do("tensor", lambda e, c1=c1, k=k, p=p, bi=bi:
                             e.matmul(psS[bi][:, 512:1024],
                                      Kf[c1][64:128, 128 * k:128 * k + 128],
                                      QP[64:128, 512 * p:512 * p + 512],
                                      start=True, stop=True), inc=sS)
                        smk = sS.mark()
                        wl = [smk]
                        if i >= 2:
                            wl.append(pvv[i - 2])
                        P.do("scalar", lambda e, bi=bi: e.activation(
                            out=EB[bi][:], in_=psS[bi][:], func=AF.Exp,
                            bias=negMc[:], scale=0.125), wait=wl, inc=sE)
                        expv[i] = sE.mark()
                        if k >= NK - 8:
                            tix = k - (NK - 8)
                            msrc = bass.AP(masksb, 512 * tix,
                                           [[4096, 128], [0, 2], [1, 512]])
                            P.do("vector", lambda e, bi=bi, msrc=msrc: e.tensor_mul(
                                out=EB[bi][:], in0=EB[bi][:], in1=msrc),
                                wait=[expv[i]], inc=sM)
                            maskv[i] = sM.mark()
                        if pending_dve:
                            pending_dve.pop(0)()
                        if i >= 1:
                            p2, k2 = iters[i - 1]
                            sched_pv(i - 1, p2, k2)
                            if k2 == NK - 1:
                                sched_epilogue(p2)
                    sched_pv(len(iters) - 1, 15, NK - 1)
                    sched_epilogue(15)
                    while pending_dve:
                        pending_dve.pop(0)()
                    P.run(nc, f"gb{j}")

                with contextlib.ExitStack() as ast:
                    psA = [ast.enter_context(
                        nc.psum_tensor(f"psA{j}{i}", [128, 512], F32))
                        for i in range(8)]
                    P = Phase()
                    wst_free = [None] * 4
                    psA_free = [None] * 8

                    def proj_stream(weight_ap, lhs_buf, bias_add, ready_marks,
                                    first_load_wait=None):
                        tile_done = [None] * 4
                        for och in range(2):
                            for afc in range(16):
                                wi = afc % 4
                                P.do("sync", lambda e, afc=afc, och=och, wi=wi,
                                     w=weight_ap: e.dma_start(
                                         out=WST[wi][:, 0:1024],
                                         in_=w[128 * afc:128 * afc + 128,
                                               1024 * och:1024 * och + 1024]),
                                     wait=[wst_free[wi],
                                           (first_load_wait
                                            if (och == 0 and afc == 0) else None)],
                                     inc=sW, inck=16)
                                wmk = sW.mark()
                                stop = (afc == 15) and not bias_add
                                for bidx in range(8):
                                    t, o2 = bidx // 2, bidx % 2
                                    if afc == 0:
                                        wl = [wmk, psA_free[bidx]] + ready_marks
                                    else:
                                        wl = [wmk] if bidx == 0 else []
                                    P.do("tensor", lambda e, afc=afc, t=t, o2=o2,
                                         wi=wi, bidx=bidx, stop=stop:
                                         e.matmul(psA[bidx][:],
                                                  lhs_buf[:, 512 * afc + 128 * t:
                                                          512 * afc + 128 * t + 128],
                                                  WST[wi][:, 512 * o2:512 * o2 + 512],
                                                  start=(afc == 0), stop=stop),
                                         wait=wl,
                                         inc=(sPE if bidx == 7 else None))
                                wst_free[wi] = sPE.mark()
                            if bias_add:
                                for bidx in range(8):
                                    t, o2 = bidx // 2, bidx % 2
                                    col = 1024 * och + 512 * o2
                                    P.do("tensor", lambda e, bidx=bidx, col=col:
                                         e.matmul(psA[bidx][:],
                                                  onesrow[0:1, :],
                                                  browb[0:1, col:col + 512],
                                                  start=False, stop=True),
                                         inc=(sPE if bidx == 7 else None))
                            pe_mk = sPE.mark()
                            for bidx in range(8):
                                t, o2 = bidx // 2, bidx % 2
                                col = 1024 * och + 512 * o2
                                P.do("vector", lambda e, bidx=bidx, t=t, col=col:
                                     e.tensor_add(
                                         out=XG[t][:, col:col + 512],
                                         in0=psA[bidx][:],
                                         in1=XG[t][:, col:col + 512]),
                                     wait=[pe_mk], inc=sVe)
                                psA_free[bidx] = sVe.mark()
                                if och == 1:
                                    tile_done[t] = sVe.mark()
                        return tile_done

                    of_ready = sVo.mark()
                    x2_done = proj_stream(
                        wobf, OF, False, [of_ready],
                        first_load_wait=(prep_stores if j == 0 else None))

                    psT_free2 = [psA_free[0], psA_free[1]]
                    xh_free = [None, None]
                    pend = None
                    for t in range(5):
                        if t < 4:
                            fm = ln_front(P, stats[t], XG[t][:], [x2_done[t]])
                            nxt = (t, fm)
                        else:
                            nxt = None
                        if pend is not None:
                            t0, fm0 = pend
                            xh_mk = ln_back(P, stats[t0], XG[t0][:], XH[t0 % 2][:],
                                            fm0, dst_free=xh_free[t0 % 2])
                            _, last_pe = transpose_tile(P, XH[t0 % 2], t0, XF,
                                                        [psA[0], psA[1]],
                                                        psT_free2, xh_mk)
                            xh_free[t0 % 2] = last_pe
                        pend = nxt
                    xf2_ready = sVe.mark()
                    psA_free[0] = psT_free2[0]
                    psA_free[1] = psT_free2[1]

                    y_done = proj_stream(wfbf, XF, True, [xf2_ready])

                    for t in range(4):
                        P.do("sync", lambda e, t=t, j=j: e.dma_start(
                            out=out_d[512 * j + 128 * t:512 * j + 128 * t + 128, :],
                            in_=XG[t][:]), wait=[y_done[t]], inc=sOUT, inck=16)
                        xg_store[t] = sOUT.mark()
                    P.run(nc, f"gcd{j}")

            P = Phase()
            P.do("gpsimd", lambda e: e.engine_nop(), wait=[sOUT.mark()])
            P.run(nc, "fin")

    return nc


def _make_masks(par):
    m = np.zeros((8, 128, 512), dtype=np.float32)
    ii = np.arange(128)[:, None]
    jj = np.arange(512)[None, :]
    for t in range(8):
        if par == 0:
            if t < 4:
                m[t] = (128 * t + ii <= jj).astype(np.float32)
        else:
            if t < 4:
                m[t] = 1.0
            else:
                m[t] = (128 * (t - 4) + ii <= jj).astype(np.float32)
    return m.astype(ml_dtypes.bfloat16)


def kernel(L0_attn_x, L0_attn_Wq, L0_attn_Wk, L0_attn_Wv, L0_attn_Wo,
           L0_ffn_weight, L0_ffn_scale_weight, L0_ffn_bias):
    x = np.ascontiguousarray(np.asarray(L0_attn_x, dtype=np.float32))
    wq = np.ascontiguousarray(np.asarray(L0_attn_Wq, dtype=np.float32))
    wk = np.ascontiguousarray(np.asarray(L0_attn_Wk, dtype=np.float32))
    wv = np.ascontiguousarray(np.asarray(L0_attn_Wv, dtype=np.float32))
    wo = np.ascontiguousarray(np.asarray(L0_attn_Wo, dtype=np.float32))
    wf = np.ascontiguousarray(np.asarray(L0_ffn_weight, dtype=np.float32))
    sc = np.ascontiguousarray(np.asarray(L0_ffn_scale_weight, dtype=np.float32))
    bi = np.ascontiguousarray(np.asarray(L0_ffn_bias, dtype=np.float32))
    ident = np.eye(128, dtype=ml_dtypes.bfloat16)
    masks = [_make_masks(0), _make_masks(1)]

    nc = build_program()
    in_maps = []
    for c in range(8):
        b, par = c // 2, c % 2
        xq_c = np.ascontiguousarray(np.concatenate(
            [x[b, 1024 * j + 512 * par:1024 * j + 512 * par + 512]
             for j in range(4)], axis=0))
        in_maps.append({
            "xfull": np.ascontiguousarray(x[b]),
            "xq": xq_c,
            "wq": wq, "wk": wk, "wv": wv, "wo": wo,
            "wffn": wf, "scale": sc, "bias": bi,
            "masks": masks[par], "ident": ident,
        })
    res = run_bass_kernel_spmd(nc, in_maps, core_ids=list(range(8)))

    out = np.empty((B, S, D), dtype=np.float32)
    for c in range(8):
        b, par = c // 2, c % 2
        oc = res.results[c]["out"]
        for j in range(4):
            out[b, 1024 * j + 512 * par:1024 * j + 512 * par + 512] = \
                oc[512 * j:512 * j + 512]
    return out


if __name__ == "__main__":
    prog = build_program()
    print("build ok, instructions:", prog.n_instructions())


# version 14
# speedup vs baseline: 1.0008x; 1.0008x over previous
import sys
sys.path.insert(0, '/opt/trn_rl_repo')
import contextlib
import numpy as np
import ml_dtypes
import concourse.bass as bass
import concourse.mybir as mybir
from concourse.bass_utils import run_bass_kernel_spmd

F32 = mybir.dt.float32
BF16 = mybir.dt.bfloat16
AF = mybir.ActivationFunctionType
ALU = mybir.AluOpType
AX = mybir.AxisListType

B, S, D = 4, 4096, 2048
NIC = D // 128
NEG_SHIFT = -12.0
EPS = 1e-5
ENGS = ("sync", "tensor", "vector", "scalar", "gpsimd")


class Sem:
    def __init__(self, handle):
        self.h = handle
        self.v = 0

    def mark(self):
        return (self, self.v)


class Phase:

    def __init__(self):
        self.q = {e: [] for e in ENGS}

    def do(self, eng, fn, wait=(), inc=None, inck=1):
        waits = [w for w in wait if w is not None]
        if inc is not None:
            inc.v += inck
        self.q[eng].append((fn, waits, inc, inck))

    def run(self, nc, name):
        with nc.Block(name) as blk:
            def register(eng_name):
                def body(eng):
                    for fn, waits, inc, inck in self.q[eng_name]:
                        for s, val in waits:
                            if val > 0:
                                eng.wait_ge(s.h, val)
                        ins = fn(eng)
                        if inc is not None:
                            ins.then_inc(inc.h, inck)
                return body
            for e in ENGS:
                if self.q[e]:
                    getattr(blk, e)(register(e))


def pair_heads(p):
    h1 = 8 * (p // 4) + (p % 4)
    return h1, h1 + 4


def build_program():
    nc = bass.Bass("TRN2", target_bir_lowering=False)

    xfull = nc.dram_tensor("xfull", [S, D], F32, kind="ExternalInput").ap()
    xq = nc.dram_tensor("xq", [2048, D], F32, kind="ExternalInput").ap()
    wq_in = nc.dram_tensor("wq", [D, D], F32, kind="ExternalInput").ap()
    wk_in = nc.dram_tensor("wk", [D, 512], F32, kind="ExternalInput").ap()
    wv_in = nc.dram_tensor("wv", [D, 512], F32, kind="ExternalInput").ap()
    wo_in = nc.dram_tensor("wo", [D, D], F32, kind="ExternalInput").ap()
    wf_in = nc.dram_tensor("wffn", [D, D], F32, kind="ExternalInput").ap()
    sc_in = nc.dram_tensor("scale", [D], F32, kind="ExternalInput")
    bi_in = nc.dram_tensor("bias", [D], F32, kind="ExternalInput").ap()
    mk_in = nc.dram_tensor("masks", [8, 128, 512], BF16, kind="ExternalInput").ap()
    id_in = nc.dram_tensor("ident", [128, 128], BF16, kind="ExternalInput").ap()
    out_d = nc.dram_tensor("out", [2048, D], F32, kind="ExternalOutput").ap()

    wqbf_h = nc.dram_tensor("wq_bf", [16, 128, NIC * 128], BF16, kind="Internal")
    wobf = nc.dram_tensor("wo_bf", [D, D], BF16, kind="Internal").ap()
    wfbf_h = nc.dram_tensor("wfT_bf", [D, D], BF16, kind="Internal")
    rb_h = nc.dram_tensor("rbounce", [2, 2048], F32, kind="Internal")
    wqbf = wqbf_h.ap()
    wfbf = wfbf_h.ap()

    gst = contextlib.ExitStack()
    with gst:
        ident = gst.enter_context(nc.sbuf_tensor("identsb", [128, 128], BF16))
        masksb = gst.enter_context(nc.sbuf_tensor("masksb", [128, 4096], BF16))
        onesrow = gst.enter_context(nc.sbuf_tensor("onesrow", [1, 128], BF16))
        negMc = gst.enter_context(nc.sbuf_tensor("negMc", [128, 1], F32))
        epsc = gst.enter_context(nc.sbuf_tensor("epsc", [128, 1], F32))
        browb = gst.enter_context(nc.sbuf_tensor("browb", [1, D], BF16))
        scol = gst.enter_context(nc.sbuf_tensor("scol", [128, 16], F32))
        Kf = [gst.enter_context(nc.sbuf_tensor(f"Kf{c}", [128, S], BF16))
              for c in range(4)]
        Vt = [gst.enter_context(nc.sbuf_tensor(f"Vt{g}", [128, 520], BF16))
              for g in range(32)]
        stats = [gst.enter_context(nc.sbuf_tensor(f"stats{t}", [128, 12], F32))
                 for t in range(4)]
        XH = [gst.enter_context(nc.sbuf_tensor(f"XHb{i}", [128, D], BF16))
              for i in range(2)]
        XSQ = gst.enter_context(nc.sbuf_tensor("XSQ", [128, D], BF16))
        CFV = gst.enter_context(nc.sbuf_tensor("CFV", [128, 2], F32))
        CFG = gst.enter_context(nc.sbuf_tensor("CFG", [128, 2], F32))
        CFA = gst.enter_context(nc.sbuf_tensor("CFA", [128, 2], F32))

        sm = {n: Sem(gst.enter_context(nc.semaphore(n)))
              for n in ("sX", "sW", "sSC", "sRB", "sOUT", "sPE", "sS", "sPV",
                        "sA", "sE", "sV", "sV2", "sVe", "sVr", "sVo", "sM")}
        sX, sW, sSC, sRB, sOUT = (sm[k] for k in ("sX", "sW", "sSC", "sRB", "sOUT"))
        sPE, sS, sPV = sm["sPE"], sm["sS"], sm["sPV"]
        sA, sE = sm["sA"], sm["sE"]
        sV, sV2, sVe, sVr, sVo = (sm[k] for k in ("sV", "sV2", "sVe", "sVr", "sVo"))
        sM = sm["sM"]

        def confirm(P, eng, sem):
            buf = {"vector": CFV, "gpsimd": CFG, "scalar": CFA}[eng]
            if eng == "scalar":
                P.do(eng, lambda e: e.copy(out=buf[:, 0:1], in_=buf[:, 1:2]),
                     inc=sem)
            else:
                P.do(eng, lambda e: e.tensor_copy(out=buf[:, 0:1], in_=buf[:, 1:2]),
                     inc=sem)
            return sem.mark()

        def ln_front(P, stt, src, src_wait):
            P.do("scalar", lambda a: a.activation(
                out=XSQ[:], in_=src, func=AF.Square, accum_out=stt[:, 1:2]),
                wait=src_wait)
            sq_mk = confirm(P, "scalar", sA)
            P.do("vector", lambda v: v.reduce_sum(out=stt[:, 0:1], in_=src,
                                                  axis=AX.X), wait=src_wait)
            rs_mk = confirm(P, "vector", sV2)
            P.do("gpsimd", lambda g: g.tensor_scalar(
                out=stt[:, 2:3], in0=stt[:, 0:1], scalar1=1.0 / D, scalar2=None,
                op0=ALU.mult), wait=[rs_mk], inc=sM)
            m1 = sM.mark()
            P.do("gpsimd", lambda g: g.tensor_scalar(
                out=stt[:, 3:4], in0=stt[:, 1:2], scalar1=1.0 / D, scalar2=None,
                op0=ALU.mult), wait=[sq_mk])
            P.do("gpsimd", lambda g: g.tensor_mul(
                out=stt[:, 6:7], in0=stt[:, 2:3], in1=stt[:, 2:3]),
                wait=[m1], inc=sM)
            m2 = sM.mark()
            P.do("gpsimd", lambda g: g.tensor_sub(
                out=stt[:, 3:4], in0=stt[:, 3:4], in1=stt[:, 6:7]),
                wait=[m2])
            P.do("gpsimd", lambda g: g.tensor_scalar(
                out=stt[:, 5:6], in0=stt[:, 2:3], scalar1=-1.0, scalar2=None,
                op0=ALU.mult))
            var_mk = negm_mk = confirm(P, "gpsimd", sM)
            P.do("scalar", lambda a: a.activation(
                out=stt[:, 6:7], in_=stt[:, 3:4], func=AF.Ln, bias=epsc[:]),
                wait=[var_mk], inc=sA)
            wa = sA.mark()
            P.do("scalar", lambda a: a.activation(
                out=stt[:, 4:5], in_=stt[:, 6:7], func=AF.Exp, scale=-0.5),
                wait=[wa])
            rstd_mk = confirm(P, "scalar", sA)
            return (negm_mk, rstd_mk)

        def ln_back(P, stt, src, dst, fm, dst_free=None):
            negm_mk, rstd_mk = fm
            P.do("vector", lambda v: v.tensor_scalar(
                out=dst, in0=src, scalar1=stt[:, 5:6], scalar2=stt[:, 4:5],
                op0=ALU.add, op1=ALU.mult),
                wait=[negm_mk, rstd_mk, dst_free])
            return confirm(P, "vector", sV)

        def transpose_tile(P, srcT, t, dstbuf, psTb, psT_free, ready_mark,
                           extra_wait=None):
            last_pe = None
            for q in range(4):
                qi = (4 * t + q) % 2
                for u in range(4):
                    icc = 4 * q + u
                    P.do("tensor", lambda e, qi=qi, u=u, icc=icc:
                         e.matmul(psTb[qi][:, 128 * u:128 * u + 128],
                                  srcT[:, 128 * icc:128 * icc + 128],
                                  ident[:], start=True, stop=True),
                         wait=([ready_mark, psT_free[qi], extra_wait]
                               if u == 0 else []),
                         inc=(sPE if u == 3 else None))
                pe_mk = sPE.mark()
                dst = bass.AP(dstbuf, 2048 * q + 128 * t,
                              [[16 * 512, 128], [512, 4], [1, 128]])
                src3 = bass.AP(psTb[qi], 0, [[512, 128], [128, 4], [1, 128]])
                P.do("vector", lambda e, dst=dst, src3=src3: e.tensor_copy(
                    out=dst, in_=src3), wait=[pe_mk], inc=sVe)
                psT_free[qi] = sVe.mark()
                last_pe = pe_mk
            cmk = confirm(P, "vector", sVe)
            return cmk, last_pe

        P = Phase()
        P.do("sync", lambda e: e.dma_start(out=ident[:], in_=id_in),
             inc=sX, inck=16)
        for t in range(8):
            P.do("sync", lambda e, t=t: e.dma_start(
                out=masksb[:, 512 * t:512 * t + 512], in_=mk_in[t]),
                inc=sX, inck=16)
        for c in range(16):
            P.do("sync", lambda e, c=c: e.dma_start(
                out=scol[:, c:c + 1],
                in_=bass.AP(sc_in, 128 * c, [[1, 128], [1, 1]])),
                inc=sX, inck=16)
        P.do("vector", lambda e: e.memset(onesrow[:], 1.0))
        P.do("vector", lambda e: e.memset(negMc[:], NEG_SHIFT))
        P.do("vector", lambda e: e.memset(epsc[:], EPS))
        for g in range(32):
            P.do("gpsimd", lambda e, g=g: e.memset(Vt[g][:], 1.0))
        P.do("gpsimd", lambda e: e.engine_nop(), wait=[sX.mark()])
        P.run(nc, "init")

        with contextlib.ExitStack() as pst:
            WFST = [pst.enter_context(nc.sbuf_tensor(f"WFSp{i}", [128, D], F32))
                    for i in range(2)]
            BST = [pst.enter_context(nc.sbuf_tensor(f"BSTp{i}", [128, D], BF16))
                   for i in range(2)]
            QRB = pst.enter_context(nc.sbuf_tensor("QRB", [128, 4 * D], BF16))
            FST = [pst.enter_context(nc.sbuf_tensor(f"FSTp{i}", [128, 512], BF16))
                   for i in range(2)]
            dsc = pst.enter_context(nc.sbuf_tensor("dsc", [128, D], BF16))
            psT = [pst.enter_context(nc.psum_tensor(f"psTp{i}", [128, 512], F32))
                   for i in range(2)]
            WFS1 = [pst.enter_context(nc.sbuf_tensor(f"WFS1{i}", [128, 512], F32))
                    for i in range(2)]
            WKB = pst.enter_context(nc.sbuf_tensor("WKB", [128, 16 * 512], BF16))
            WVB = pst.enter_context(nc.sbuf_tensor("WVB", [128, 16 * 512], BF16))
            xt = [pst.enter_context(nc.sbuf_tensor(f"xt{i}", [128, D], F32))
                  for i in range(2)]
            PXF = pst.enter_context(nc.sbuf_tensor("PXF", [128, 16 * 512], BF16))
            psT1 = [pst.enter_context(nc.psum_tensor(f"psT1{i}", [128, 512], F32))
                    for i in range(2)]
            pkv = [pst.enter_context(nc.psum_tensor(f"pkv{i}", [128, 512], F32))
                   for i in range(2)]

            P = Phase()
            P.do("sync", lambda e: e.dma_start(out=WFST[0][0:1, 0:D], in_=bi_in),
                 inc=sW, inck=16)
            P.do("vector", lambda e: e.tensor_copy(out=browb[:],
                                                   in_=WFST[0][0:1, 0:D]),
                 wait=[sW.mark()], inc=sM)
            wfst_free_init = sM.mark()
            for c in range(16):
                P.do("gpsimd", lambda e, c=c: e.tensor_scalar(
                    out=dsc[:, 128 * c:128 * c + 128], in0=ident[:],
                    scalar1=scol[:, c:c + 1], scalar2=None, op0=ALU.mult))
            dsc_done = confirm(P, "gpsimd", sM)
            wfs_free = [None, None]
            for sl in range(32):
                i = sl % 2
                src = wk_in if sl < 16 else wv_in
                dstT = WKB if sl < 16 else WVB
                ic = sl % 16
                P.do("sync", lambda e, src=src, ic=ic, i=i: e.dma_start(
                    out=WFS1[i][:], in_=src[128 * ic:128 * ic + 128, :]),
                    wait=[wfs_free[i]], inc=sW, inck=16)
                P.do("vector", lambda e, dstT=dstT, ic=ic, i=i: e.tensor_copy(
                    out=dstT[:, 512 * ic:512 * ic + 512], in_=WFS1[i][:]),
                    wait=[sW.mark()], inc=sM)
                wfs_free[i] = sM.mark()
            wkv_cast = confirm(P, "vector", sM)

            wfst_free = [None, None]
            bst_free = [None, None]
            psTp_free = [None, None]
            qrb_free = [None]
            fst_free = [None, None]
            xt_free = [None, None]
            psT_free = [None, None]
            pkv_free = [None, None]
            xh_free = [None, None]
            pxf_free = None
            pend = None

            def emit_prep_slab(sl):
                kind = "wq" if sl < 16 else ("wo" if sl < 32 else "wf")
                ic = sl % 16
                src = {"wq": wq_in, "wo": wo_in, "wf": wf_in}[kind]
                i = sl % 2
                ib = sl % 2
                P.do("sync", lambda e, src=src, ic=ic, i=i: e.dma_start(
                    out=WFST[i][:], in_=src[128 * ic:128 * ic + 128, :]),
                    wait=[wfst_free[i] or wfst_free_init], inc=sW, inck=16)
                if kind != "wq":
                    P.do("vector", lambda e, i=i, ib=ib: e.tensor_copy(
                        out=BST[ib][:], in_=WFST[i][:]),
                        wait=[sW.mark(), bst_free[ib]])
                    wfst_free[i] = cast_mk = confirm(P, "vector", sM)
                if kind == "wq":
                    q4 = ic % 4
                    for p in range(16):
                        h1, _ = pair_heads(p)
                        srcp = bass.AP(WFST[i], 64 * h1,
                                       [[D, 128], [256, 2], [1, 64]])
                        P.do("vector", lambda e, srcp=srcp, q4=q4, p=p:
                             e.tensor_copy(
                                 out=QRB[:, q4 * D + 128 * p:q4 * D + 128 * p + 128],
                                 in_=srcp),
                            wait=([sW.mark(), qrb_free[0]] if p == 0 else []))
                    wfst_free[i] = cast_mk = confirm(P, "vector", sM)
                    if q4 == 3:
                        for p in range(16):
                            icq = ic - 3
                            dst = bass.AP(wqbf_h,
                                          (p * 128) * (NIC * 128) + icq * 128,
                                          [[NIC * 128, 128], [128, 4],
                                           [1, 128]])
                            srcq = bass.AP(QRB, 128 * p,
                                           [[4 * D, 128], [D, 4], [1, 128]])
                            eng = "scalar" if p % 2 == 0 else "sync"
                            P.do(eng, lambda e, dst=dst, srcq=srcq: e.dma_start(
                                out=dst, in_=srcq),
                                wait=[cast_mk], inc=sSC, inck=16)
                        qrb_free[0] = sSC.mark()
                elif kind == "wo":
                    P.do("scalar", lambda e, ic=ic, ib=ib: e.dma_start(
                        out=wobf[128 * ic:128 * ic + 128, :], in_=BST[ib][:]),
                        wait=[cast_mk], inc=sSC, inck=16)
                    bst_free[ib] = sSC.mark()
                else:
                    oc = ic
                    for q in range(4):
                        qi = q % 2
                        for u in range(4):
                            icc = 4 * q + u
                            P.do("tensor", lambda e, ib=ib, qi=qi, u=u, icc=icc,
                                 oc=oc: e.matmul(
                                     psT[qi][:, 128 * u:128 * u + 128],
                                     BST[ib][:, 128 * icc:128 * icc + 128],
                                     dsc[:, 128 * oc:128 * oc + 128],
                                     start=True, stop=True),
                                 wait=([cast_mk, dsc_done, psTp_free[qi]]
                                       if u == 0 else []),
                                 inc=(sPE if u == 3 else None))
                        pe_mk = sPE.mark()
                        P.do("vector", lambda e, qi=qi: e.tensor_copy(
                            out=FST[qi][:], in_=psT[qi][:]),
                            wait=[pe_mk, fst_free[qi]])
                        psTp_free[qi] = confirm(P, "vector", sV)
                        dstw = bass.AP(wfbf_h, (512 * q) * D + 128 * oc,
                                       [[D, 128], [128 * D, 4], [1, 128]])
                        P.do("scalar", lambda e, qi=qi, dstw=dstw: e.dma_start(
                            out=dstw, in_=FST[qi][:]),
                            wait=[sV.mark()], inc=sSC, inck=16)
                        fst_free[qi] = sSC.mark()
                    bst_free[ib] = sPE.mark()

            def emit_kv_tile(kt):
                nonlocal pend, pxf_free
                if kt < 32:
                    i = kt % 2
                    P.do("sync", lambda e, kt=kt, i=i: e.dma_start(
                        out=xt[i][:], in_=xfull[128 * kt:128 * kt + 128, :]),
                        wait=[xt_free[i]], inc=sX, inck=16)
                    xmk = sX.mark()
                    fm = ln_front(P, stats[kt % 4], xt[i][:], [xmk])
                    nxt = (kt, i, kt % 2, fm)
                else:
                    nxt = None
                if pend is not None:
                    kt0, i0, ih0, fm0 = pend
                    sg, v = kt0 // 4, kt0 % 4
                    xh_mk = ln_back(P, stats[kt0 % 4], xt[i0][:], XH[ih0][:],
                                    fm0, dst_free=xh_free[ih0])
                    xt_free[i0] = xh_mk
                    _, last_pe = transpose_tile(P, XH[ih0], v, PXF, psT1,
                                                psT_free, xh_mk,
                                                extra_wait=pxf_free)
                    xh_free[ih0] = last_pe
                    if v == 3:
                        xf_ready = sVe.mark()
                        for g in range(8):
                            gi = g % 2
                            if g < 4:
                                kfc = g
                                for icx in range(NIC):
                                    P.do("tensor", lambda e, gi=gi, icx=icx,
                                         kfc=kfc: e.matmul(
                                             pkv[gi][:],
                                             WKB[:, 512 * icx + 128 * kfc:
                                                 512 * icx + 128 * kfc + 128],
                                             PXF[:, 512 * icx:512 * icx + 512],
                                             start=(icx == 0), stop=(icx == 15)),
                                         wait=([xf_ready, wkv_cast, pkv_free[gi]]
                                               if icx == 0 else []),
                                         inc=(sPE if icx == 15 else None))
                                pe_mk = sPE.mark()
                                P.do("vector", lambda e, gi=gi, kfc=kfc, sg=sg:
                                     e.tensor_copy(
                                         out=Kf[kfc][:, 512 * sg:512 * sg + 512],
                                         in_=pkv[gi][:]), wait=[pe_mk], inc=sVe)
                            else:
                                vtile = g - 4
                                for icx in range(NIC):
                                    P.do("tensor", lambda e, gi=gi, icx=icx,
                                         vt2=vtile: e.matmul(
                                             pkv[gi][:],
                                             PXF[:, 512 * icx + 128 * vt2:
                                                 512 * icx + 128 * vt2 + 128],
                                             WVB[:, 512 * icx:512 * icx + 512],
                                             start=(icx == 0), stop=(icx == 15)),
                                         wait=([pkv_free[gi]] if icx == 0 else []),
                                         inc=(sPE if icx == 15 else None))
                                pe_mk = sPE.mark()
                                dstv = bass.AP(Vt[4 * sg + vtile], 0,
                                               [[520, 128], [65, 8], [1, 64]])
                                srcp = bass.AP(pkv[gi], 0,
                                               [[512, 128], [64, 8], [1, 64]])
                                P.do("vector", lambda e, dstv=dstv, srcp=srcp:
                                     e.tensor_copy(out=dstv, in_=srcp),
                                     wait=[pe_mk], inc=sVe)
                            pkv_free[gi] = sVe.mark()
                        pxf_free = sPE.mark()
                pend = nxt

            for step in range(48):
                emit_prep_slab(step)
                if step < 33:
                    emit_kv_tile(step)
            prep_stores = sSC.mark()
            P.run(nc, "prepkv")


        with contextlib.ExitStack() as pst:
            XG = [pst.enter_context(nc.sbuf_tensor(f"XG{t}", [128, D], F32))
                  for t in range(4)]
            XF = pst.enter_context(nc.sbuf_tensor("XF", [128, 16 * 512], BF16))
            QP = pst.enter_context(nc.sbuf_tensor("QP", [128, 16 * 512], BF16))
            OF = pst.enter_context(nc.sbuf_tensor("OF", [128, 16 * 512], BF16))
            EB = [pst.enter_context(nc.sbuf_tensor(f"EB{i}", [128, 1024], BF16))
                  for i in range(2)]
            WST = [pst.enter_context(nc.sbuf_tensor(f"WST{i}", [128, D], BF16))
                   for i in range(4)]
            OSTG = [pst.enter_context(nc.sbuf_tensor(f"OSTG{i}", [65, 1024], F32))
                    for i in range(2)]
            RBa = [pst.enter_context(nc.sbuf_tensor(f"RBa{i}", [64, 512], F32))
                   for i in range(2)]
            RBb = [pst.enter_context(nc.sbuf_tensor(f"RBb{i}", [64, 512], F32))
                   for i in range(2)]
            RR1 = [pst.enter_context(nc.sbuf_tensor(f"RR1{i}", [64, 16], F32))
                   for i in range(2)]
            RR2 = [pst.enter_context(nc.sbuf_tensor(f"RR2{i}", [64, 16], F32))
                   for i in range(2)]

            xg_store = [None] * 4

            def emit_a(P, j, psTa, psQ, psT_free, psQ_free, wst_free, xh_free,
                       first_wait=None):
                pend = None
                for t in range(5):
                    if t < 4:
                        P.do("sync", lambda e, t=t, j=j: e.dma_start(
                            out=XG[t][:],
                            in_=xq[512 * j + 128 * t:512 * j + 128 * t + 128, :]),
                            wait=[xg_store[t]], inc=sX, inck=16)
                        xmk = sX.mark()
                        fm = ln_front(P, stats[t], XG[t][:], [xmk])
                        nxt = (t, fm)
                    else:
                        nxt = None
                    if pend is not None:
                        t0, fm0 = pend
                        xh_mk = ln_back(P, stats[t0], XG[t0][:], XH[t0 % 2][:],
                                        fm0, dst_free=xh_free[t0 % 2])
                        _, last_pe = transpose_tile(P, XH[t0 % 2], t0, XF,
                                                    psTa, psT_free, xh_mk)
                        xh_free[t0 % 2] = last_pe
                    pend = nxt
                xf_ready = sVe.mark()
                for p in range(16):
                    wi = p % 4
                    P.do("sync", lambda e, p=p, wi=wi: e.dma_start(
                        out=WST[wi][:], in_=wqbf[p]),
                        wait=[wst_free[wi],
                              (first_wait if p == 0 else None)],
                        inc=sW, inck=16)
                    wmk = sW.mark()
                    pi = p % 2
                    for icx in range(NIC):
                        P.do("tensor", lambda e, pi=pi, wi=wi, icx=icx:
                             e.matmul(psQ[pi][:],
                                      WST[wi][:, 128 * icx:128 * icx + 128],
                                      XF[:, 512 * icx:512 * icx + 512],
                                      start=(icx == 0), stop=(icx == 15)),
                             wait=([wmk, xf_ready, psQ_free[pi]]
                                   if icx == 0 else []),
                             inc=(sPE if icx == 15 else None))
                    wst_free[wi] = sPE.mark()
                    pe_mk = sPE.mark()
                    P.do("vector", lambda e, pi=pi, p=p: e.tensor_copy(
                        out=QP[:, 512 * p:512 * p + 512], in_=psQ[pi][:]),
                        wait=[pe_mk], inc=sVe)
                    psQ_free[pi] = sVe.mark()

            with contextlib.ExitStack() as ast:
                psTa = [ast.enter_context(
                    nc.psum_tensor("psTa0x%d" % i, [128, 512], F32))
                    for i in range(2)]
                psQ = [ast.enter_context(
                    nc.psum_tensor("psQ0x%d" % i, [128, 512], F32))
                    for i in range(2)]
                P = Phase()
                emit_a(P, 0, psTa, psQ, [None, None], [None, None],
                       [None] * 4, [None, None], first_wait=prep_stores)
                P.run(nc, "ga0")

            for j in range(4):
                NK = 8 * (j + 1)

                with contextlib.ExitStack() as ast:
                    psS = [ast.enter_context(
                        nc.psum_tensor(f"psS{j}{i}", [128, 1024], F32))
                        for i in range(2)]
                    psO = [ast.enter_context(
                        nc.psum_tensor(f"psO{j}{i}", [128, 512], F32))
                        for i in range(4)]
                    P = Phase()
                    expv, maskv, pvv, psofree = {}, {}, {}, {}
                    evacv, outb, outb2, inb = {}, {}, {}, {}

                    def sched_pv(i2, p2, k2):
                        h1, h2 = pair_heads(p2)
                        kv1, kv2 = h1 // 4, h2 // 4
                        wl = []
                        if k2 == 0 and p2 >= 2:
                            wl.append(psofree[p2 - 2])
                        wl.append(maskv[i2] if k2 >= NK - 8 else expv[i2])
                        bi = i2 % 2
                        P.do("tensor", lambda e, p2=p2, k2=k2, kv1=kv1, bi=bi:
                             e.matmul(psO[p2 % 2][0:65, :],
                                      Vt[k2][:, 65 * kv1:65 * kv1 + 65],
                                      EB[bi][:, 0:512],
                                      start=(k2 == 0), stop=(k2 == NK - 1)),
                             wait=wl)
                        P.do("tensor", lambda e, p2=p2, k2=k2, kv2=kv2, bi=bi:
                             e.matmul(psO[2 + p2 % 2][0:65, :],
                                      Vt[k2][:, 65 * kv2:65 * kv2 + 65],
                                      EB[bi][:, 512:1024],
                                      start=(k2 == 0), stop=(k2 == NK - 1)),
                             inc=sPV)
                        pvv[i2] = sPV.mark()

                    pending_dve = []

                    def sched_epilogue(p2):
                        h1, h2 = pair_heads(p2)
                        pp = p2 % 2
                        base = pp * 2048
                        wl = [pvv[p2 * NK + NK - 1]]
                        if p2 >= 1:
                            wl.append(outb[p2 - 1])

                        def cl_copy(wl=wl, pp=pp, p2=p2):
                            wl2 = list(wl)
                            if p2 >= 2:
                                wl2.append(evacv[p2 - 2])
                            P.do("vector", lambda e: e.tensor_copy(
                                out=OSTG[pp][:, 0:512], in_=psO[pp][0:65, :]),
                                wait=wl2)
                            P.do("vector", lambda e: e.tensor_copy(
                                out=OSTG[pp][:, 512:1024],
                                in_=psO[2 + pp][0:65, :]))
                            psofree[p2] = confirm(P, "vector", sVr)
                        pending_dve.append(cl_copy)
                        svr1 = (sVr, sVr.v + 1)
                        sVr.v += 0
                        P.do("sync", lambda e, base=base, pp=pp: e.dma_start(
                            out=bass.AP(rb_h, base, [[2048, 1], [1, 1024]]),
                            in_=OSTG[pp][64:65, :]),
                            wait=[svr1], inc=sRB, inck=16)
                        outb[p2] = sRB.mark()
                        P.do("sync", lambda e, pp=pp, base=base: e.dma_start(
                            out=RR1[pp][:],
                            in_=bass.AP(rb_h, base, [[16, 64], [1, 16]])),
                            wait=[outb[p2]], inc=sRB, inck=16)
                        rr1_mk = sRB.mark()

                        def cl_recip(rr1_mk=rr1_mk, pp=pp, p2=p2):
                            P.do("vector", lambda e: e.reciprocal(
                                out=RR2[pp][:], in_=RR1[pp][:]),
                                wait=[rr1_mk,
                                      (outb2.get(p2 - 2) if p2 >= 2 else None)])
                            confirm(P, "vector", sVr)
                        pending_dve.append(cl_recip)
                        svr2 = (sVr, sVr.v + 2)
                        P.do("sync", lambda e, pp=pp, base=base: e.dma_start(
                            out=bass.AP(rb_h, base + 1024, [[16, 64], [1, 16]]),
                            in_=RR2[pp][:]),
                            wait=[svr2], inc=sRB, inck=16)
                        outb2[p2] = sRB.mark()
                        P.do("sync", lambda e, pp=pp, base=base: e.dma_start(
                            out=RBa[pp][:],
                            in_=bass.AP(rb_h, base + 1024, [[0, 64], [1, 512]])),
                            wait=[outb2[p2],
                                  (evacv[p2 - 2] if p2 >= 2 else None)],
                            inc=sRB, inck=16)
                        P.do("sync", lambda e, pp=pp, base=base: e.dma_start(
                            out=RBb[pp][:],
                            in_=bass.AP(rb_h, base + 1024 + 512,
                                        [[0, 64], [1, 512]])),
                            inc=sRB, inck=16)
                        inb[p2] = sRB.mark()
                        inb_mk = inb[p2]

                        def cl_mult(inb_mk=inb_mk, pp=pp, h1=h1, h2=h2, p2=p2):
                            for hh, col, rbuf in ((h1, 0, RBa[pp]),
                                                  (h2, 512, RBb[pp])):
                                dst = OF[64 * (hh % 2):64 * (hh % 2) + 64,
                                         512 * (hh // 2):512 * (hh // 2) + 512]
                                P.do("vector",
                                     lambda e, col=col, rbuf=rbuf, dst=dst,
                                     pp=pp: e.tensor_tensor(
                                         out=dst,
                                         in0=OSTG[pp][0:64, col:col + 512],
                                         in1=rbuf[:], op=ALU.mult),
                                     wait=[inb_mk],
                                     inc=(sVo if hh == h2 else None))
                            evacv[p2] = sVo.mark()
                        pending_dve.append(cl_mult)

                    iters = [(p, k) for p in range(16) for k in range(NK)]
                    for i, (p, k) in enumerate(iters):
                        c1 = p // 4
                        bi = i % 2
                        wl = [expv[i - 2]] if i >= 2 else []
                        P.do("tensor", lambda e, c1=c1, k=k, p=p, bi=bi:
                             e.matmul(psS[bi][:, 0:512],
                                      Kf[c1][0:64, 128 * k:128 * k + 128],
                                      QP[0:64, 512 * p:512 * p + 512],
                                      start=True, stop=True), wait=wl)
                        P.do("tensor", lambda e, c1=c1, k=k, p=p, bi=bi:
                             e.matmul(psS[bi][:, 512:1024],
                                      Kf[c1][64:128, 128 * k:128 * k + 128],
                                      QP[64:128, 512 * p:512 * p + 512],
                                      start=True, stop=True), inc=sS)
                        smk = sS.mark()
                        wl = [smk]
                        if i >= 2:
                            wl.append(pvv[i - 2])
                        P.do("scalar", lambda e, bi=bi: e.activation(
                            out=EB[bi][:], in_=psS[bi][:], func=AF.Exp,
                            bias=negMc[:], scale=0.125), wait=wl)
                        expv[i] = confirm(P, "scalar", sE)
                        if k >= NK - 8:
                            tix = k - (NK - 8)
                            msrc = bass.AP(masksb, 512 * tix,
                                           [[4096, 128], [0, 2], [1, 512]])
                            P.do("vector", lambda e, bi=bi, msrc=msrc: e.tensor_mul(
                                out=EB[bi][:], in0=EB[bi][:], in1=msrc),
                                wait=[expv[i]])
                            maskv[i] = confirm(P, "vector", sM)
                        if pending_dve:
                            pending_dve.pop(0)()
                        if i >= 1:
                            p2, k2 = iters[i - 1]
                            sched_pv(i - 1, p2, k2)
                            if k2 == NK - 1:
                                sched_epilogue(p2)
                    sched_pv(len(iters) - 1, 15, NK - 1)
                    sched_epilogue(15)
                    while pending_dve:
                        pending_dve.pop(0)()
                    P.run(nc, f"gb{j}")

                with contextlib.ExitStack() as ast:
                    psA = [ast.enter_context(
                        nc.psum_tensor(f"psA{j}{i}", [128, 512], F32))
                        for i in range(8)]
                    P = Phase()
                    wst_free = [None] * 4
                    psA_free = [None] * 8
                    def proj_stream(weight_ap, lhs_buf, bias_add, ready_marks,
                                    first_load_wait=None):
                        tile_done = [None] * 4
                        for och in range(2):
                            for afc in range(16):
                                wi = afc % 4
                                P.do("sync", lambda e, afc=afc, och=och, wi=wi,
                                     w=weight_ap: e.dma_start(
                                         out=WST[wi][:, 0:1024],
                                         in_=w[128 * afc:128 * afc + 128,
                                               1024 * och:1024 * och + 1024]),
                                     wait=[wst_free[wi],
                                           (first_load_wait
                                            if (och == 0 and afc == 0) else None)],
                                     inc=sW, inck=16)
                                wmk = sW.mark()
                                stop = (afc == 15) and not bias_add
                                for bidx in range(8):
                                    t, o2 = bidx // 2, bidx % 2
                                    if afc == 0:
                                        wl = [wmk, psA_free[bidx]] + ready_marks
                                    else:
                                        wl = [wmk] if bidx == 0 else []
                                    P.do("tensor", lambda e, afc=afc, t=t, o2=o2,
                                         wi=wi, bidx=bidx, stop=stop:
                                         e.matmul(psA[bidx][:],
                                                  lhs_buf[:, 512 * afc + 128 * t:
                                                          512 * afc + 128 * t + 128],
                                                  WST[wi][:, 512 * o2:512 * o2 + 512],
                                                  start=(afc == 0), stop=stop),
                                         wait=wl,
                                         inc=(sPE if bidx == 7 else None))
                                wst_free[wi] = sPE.mark()
                            if bias_add:
                                for bidx in range(8):
                                    t, o2 = bidx // 2, bidx % 2
                                    col = 1024 * och + 512 * o2
                                    P.do("tensor", lambda e, bidx=bidx, col=col:
                                         e.matmul(psA[bidx][:],
                                                  onesrow[0:1, :],
                                                  browb[0:1, col:col + 512],
                                                  start=False, stop=True),
                                         inc=(sPE if bidx == 7 else None))
                            pe_mk = sPE.mark()
                            for bidx in range(8):
                                t, o2 = bidx // 2, bidx % 2
                                col = 1024 * och + 512 * o2
                                P.do("vector", lambda e, bidx=bidx, t=t, col=col:
                                     e.tensor_add(
                                         out=XG[t][:, col:col + 512],
                                         in0=psA[bidx][:],
                                         in1=XG[t][:, col:col + 512]),
                                     wait=[pe_mk], inc=sVe)
                                psA_free[bidx] = sVe.mark()
                                if och == 1 and bidx % 2 == 1:
                                    tile_done[t] = confirm(P, "vector", sVe)
                        return tile_done

                    of_ready = sVo.mark()
                    x2_done = proj_stream(
                        wobf, OF, False, [of_ready],
                        first_load_wait=(prep_stores if j == 0 else None))

                    psT_free2 = [psA_free[0], psA_free[1]]
                    xh_free = [None, None]
                    pend = None
                    for t in range(5):
                        if t < 4:
                            fm = ln_front(P, stats[t], XG[t][:], [x2_done[t]])
                            nxt = (t, fm)
                        else:
                            nxt = None
                        if pend is not None:
                            t0, fm0 = pend
                            xh_mk = ln_back(P, stats[t0], XG[t0][:], XH[t0 % 2][:],
                                            fm0, dst_free=xh_free[t0 % 2])
                            _, last_pe = transpose_tile(P, XH[t0 % 2], t0, XF,
                                                        [psA[0], psA[1]],
                                                        psT_free2, xh_mk)
                            xh_free[t0 % 2] = last_pe
                        pend = nxt
                    xf2_ready = sVe.mark()
                    psA_free[0] = psT_free2[0]
                    psA_free[1] = psT_free2[1]

                    y_done = proj_stream(wfbf, XF, True, [xf2_ready])

                    for t in range(4):
                        P.do("sync", lambda e, t=t, j=j: e.dma_start(
                            out=out_d[512 * j + 128 * t:512 * j + 128 * t + 128, :],
                            in_=XG[t][:]), wait=[y_done[t]], inc=sOUT, inck=16)
                        xg_store[t] = sOUT.mark()
                    if j < 3:
                        emit_a(P, j + 1, [psA[2], psA[3]], [psA[4], psA[5]],
                               [psA_free[2], psA_free[3]],
                               [psA_free[4], psA_free[5]], wst_free, xh_free)
                    P.run(nc, f"gcd{j}")


            P = Phase()
            P.do("gpsimd", lambda e: e.engine_nop(), wait=[sOUT.mark()])
            P.run(nc, "fin")

    return nc


def _make_masks(par):
    m = np.zeros((8, 128, 512), dtype=np.float32)
    ii = np.arange(128)[:, None]
    jj = np.arange(512)[None, :]
    for t in range(8):
        if par == 0:
            if t < 4:
                m[t] = (128 * t + ii <= jj).astype(np.float32)
        else:
            if t < 4:
                m[t] = 1.0
            else:
                m[t] = (128 * (t - 4) + ii <= jj).astype(np.float32)
    return m.astype(ml_dtypes.bfloat16)


def kernel(L0_attn_x, L0_attn_Wq, L0_attn_Wk, L0_attn_Wv, L0_attn_Wo,
           L0_ffn_weight, L0_ffn_scale_weight, L0_ffn_bias):
    x = np.ascontiguousarray(np.asarray(L0_attn_x, dtype=np.float32))
    wq = np.ascontiguousarray(np.asarray(L0_attn_Wq, dtype=np.float32))
    wk = np.ascontiguousarray(np.asarray(L0_attn_Wk, dtype=np.float32))
    wv = np.ascontiguousarray(np.asarray(L0_attn_Wv, dtype=np.float32))
    wo = np.ascontiguousarray(np.asarray(L0_attn_Wo, dtype=np.float32))
    wf = np.ascontiguousarray(np.asarray(L0_ffn_weight, dtype=np.float32))
    sc = np.ascontiguousarray(np.asarray(L0_ffn_scale_weight, dtype=np.float32))
    bi = np.ascontiguousarray(np.asarray(L0_ffn_bias, dtype=np.float32))
    ident = np.eye(128, dtype=ml_dtypes.bfloat16)
    masks = [_make_masks(0), _make_masks(1)]

    nc = build_program()
    in_maps = []
    for c in range(8):
        b, par = c // 2, c % 2
        xq_c = np.ascontiguousarray(np.concatenate(
            [x[b, 1024 * j + 512 * par:1024 * j + 512 * par + 512]
             for j in range(4)], axis=0))
        in_maps.append({
            "xfull": np.ascontiguousarray(x[b]),
            "xq": xq_c,
            "wq": wq, "wk": wk, "wv": wv, "wo": wo,
            "wffn": wf, "scale": sc, "bias": bi,
            "masks": masks[par], "ident": ident,
        })
    res = run_bass_kernel_spmd(nc, in_maps, core_ids=list(range(8)))

    out = np.empty((B, S, D), dtype=np.float32)
    for c in range(8):
        b, par = c // 2, c % 2
        oc = res.results[c]["out"]
        for j in range(4):
            out[b, 1024 * j + 512 * par:1024 * j + 512 * par + 512] = \
                oc[512 * j:512 * j + 512]
    return out


if __name__ == "__main__":
    prog = build_program()
    print("build ok, instructions:", prog.n_instructions())


# version 18
# speedup vs baseline: 1.0178x; 1.0169x over previous
import sys
sys.path.insert(0, '/opt/trn_rl_repo')
import contextlib
import numpy as np
import ml_dtypes
import concourse.bass as bass
import concourse.mybir as mybir
from concourse.bass_utils import run_bass_kernel_spmd

F32 = mybir.dt.float32
BF16 = mybir.dt.bfloat16
AF = mybir.ActivationFunctionType
ALU = mybir.AluOpType
AX = mybir.AxisListType

B, S, D = 4, 4096, 2048
NIC = D // 128
NEG_SHIFT = -12.0
EPS = 1e-5
ENGS = ("sync", "tensor", "vector", "scalar", "gpsimd")


class Sem:
    def __init__(self, handle):
        self.h = handle
        self.v = 0

    def mark(self):
        return (self, self.v)


class Phase:

    def __init__(self):
        self.q = {e: [] for e in ENGS}

    def do(self, eng, fn, wait=(), inc=None, inck=1):
        waits = [w for w in wait if w is not None]
        if inc is not None:
            inc.v += inck
        self.q[eng].append((fn, waits, inc, inck))

    def run(self, nc, name):
        with nc.Block(name) as blk:
            def register(eng_name):
                def body(eng):
                    for fn, waits, inc, inck in self.q[eng_name]:
                        for s, val in waits:
                            if val > 0:
                                eng.wait_ge(s.h, val)
                        ins = fn(eng)
                        if inc is not None:
                            ins.then_inc(inc.h, inck)
                return body
            for e in ENGS:
                if self.q[e]:
                    getattr(blk, e)(register(e))


def pair_heads(p):
    h1 = 8 * (p // 4) + (p % 4)
    return h1, h1 + 4


def build_program():
    nc = bass.Bass("TRN2", target_bir_lowering=False)

    xfull = nc.dram_tensor("xfull", [S, D], F32, kind="ExternalInput").ap()
    xq = nc.dram_tensor("xq", [2048, D], F32, kind="ExternalInput").ap()
    wq_in = nc.dram_tensor("wq", [D, D], F32, kind="ExternalInput").ap()
    wk_in = nc.dram_tensor("wk", [D, 512], F32, kind="ExternalInput").ap()
    wv_in = nc.dram_tensor("wv", [D, 512], F32, kind="ExternalInput").ap()
    wo_in = nc.dram_tensor("wo", [D, D], F32, kind="ExternalInput").ap()
    wf_in = nc.dram_tensor("wffn", [D, D], F32, kind="ExternalInput").ap()
    sc_in = nc.dram_tensor("scale", [D], F32, kind="ExternalInput")
    bi_in = nc.dram_tensor("bias", [D], F32, kind="ExternalInput").ap()
    mk_in = nc.dram_tensor("masks", [8, 128, 512], BF16, kind="ExternalInput").ap()
    id_in = nc.dram_tensor("ident", [128, 128], BF16, kind="ExternalInput").ap()
    out_d = nc.dram_tensor("out", [2048, D], F32, kind="ExternalOutput").ap()

    wqbf_h = nc.dram_tensor("wq_bf", [16, 128, NIC * 128], BF16, kind="Internal")
    wobf = nc.dram_tensor("wo_bf", [D, D], BF16, kind="Internal").ap()
    wfbf_h = nc.dram_tensor("wfT_bf", [D, D], BF16, kind="Internal")
    rb_h = nc.dram_tensor("rbounce", [2, 2048], F32, kind="Internal")
    wqbf = wqbf_h.ap()
    wfbf = wfbf_h.ap()

    gst = contextlib.ExitStack()
    with gst:
        ident = gst.enter_context(nc.sbuf_tensor("identsb", [128, 128], BF16))
        masksb = gst.enter_context(nc.sbuf_tensor("masksb", [128, 4096], BF16))
        onesrow = gst.enter_context(nc.sbuf_tensor("onesrow", [1, 128], BF16))
        negMc = gst.enter_context(nc.sbuf_tensor("negMc", [128, 1], F32))
        epsc = gst.enter_context(nc.sbuf_tensor("epsc", [128, 1], F32))
        browb = gst.enter_context(nc.sbuf_tensor("browb", [1, D], BF16))
        scol = gst.enter_context(nc.sbuf_tensor("scol", [128, 16], F32))
        Kf = [gst.enter_context(nc.sbuf_tensor(f"Kf{c}", [128, S], BF16))
              for c in range(4)]
        Vt = [gst.enter_context(nc.sbuf_tensor(f"Vt{g}", [128, 520], BF16))
              for g in range(32)]
        stats = [gst.enter_context(nc.sbuf_tensor(f"stats{t}", [128, 12], F32))
                 for t in range(4)]
        XH = [gst.enter_context(nc.sbuf_tensor(f"XHb{i}", [128, D], BF16))
              for i in range(2)]
        XSQ = gst.enter_context(nc.sbuf_tensor("XSQ", [128, D], BF16))
        CFV = gst.enter_context(nc.sbuf_tensor("CFV", [128, 2], F32))
        CFG = gst.enter_context(nc.sbuf_tensor("CFG", [128, 2], F32))
        CFA = gst.enter_context(nc.sbuf_tensor("CFA", [128, 2], F32))

        sm = {n: Sem(gst.enter_context(nc.semaphore(n)))
              for n in ("sX", "sW", "sSC", "sRB", "sOUT", "sPE", "sS", "sPV",
                        "sA", "sE", "sV", "sV2", "sVe", "sVr", "sVo", "sM")}
        sX, sW, sSC, sRB, sOUT = (sm[k] for k in ("sX", "sW", "sSC", "sRB", "sOUT"))
        sPE, sS, sPV = sm["sPE"], sm["sS"], sm["sPV"]
        sA, sE = sm["sA"], sm["sE"]
        sV, sV2, sVe, sVr, sVo = (sm[k] for k in ("sV", "sV2", "sVe", "sVr", "sVo"))
        sM = sm["sM"]

        def confirm(P, eng, sem):
            buf = {"vector": CFV, "gpsimd": CFG, "scalar": CFA}[eng]
            if eng == "scalar":
                P.do(eng, lambda e: e.copy(out=buf[:, 0:1], in_=buf[:, 1:2]),
                     inc=sem)
            else:
                P.do(eng, lambda e: e.tensor_copy(out=buf[:, 0:1], in_=buf[:, 1:2]),
                     inc=sem)
            return sem.mark()

        def ln_front(P, stt, src, src_wait):
            P.do("scalar", lambda a: a.activation(
                out=XSQ[:], in_=src, func=AF.Square, accum_out=stt[:, 1:2]),
                wait=src_wait)
            sq_mk = confirm(P, "scalar", sA)
            P.do("vector", lambda v: v.reduce_sum(out=stt[:, 0:1], in_=src,
                                                  axis=AX.X), wait=src_wait)
            rs_mk = confirm(P, "vector", sV2)
            P.do("gpsimd", lambda g: g.tensor_scalar(
                out=stt[:, 2:3], in0=stt[:, 0:1], scalar1=1.0 / D, scalar2=None,
                op0=ALU.mult), wait=[rs_mk], inc=sM)
            m1 = sM.mark()
            P.do("gpsimd", lambda g: g.tensor_scalar(
                out=stt[:, 3:4], in0=stt[:, 1:2], scalar1=1.0 / D, scalar2=None,
                op0=ALU.mult), wait=[sq_mk])
            P.do("gpsimd", lambda g: g.tensor_mul(
                out=stt[:, 6:7], in0=stt[:, 2:3], in1=stt[:, 2:3]),
                wait=[m1], inc=sM)
            m2 = sM.mark()
            P.do("gpsimd", lambda g: g.tensor_sub(
                out=stt[:, 3:4], in0=stt[:, 3:4], in1=stt[:, 6:7]),
                wait=[m2])
            P.do("gpsimd", lambda g: g.tensor_scalar(
                out=stt[:, 5:6], in0=stt[:, 2:3], scalar1=-1.0, scalar2=None,
                op0=ALU.mult))
            var_mk = negm_mk = confirm(P, "gpsimd", sM)
            P.do("scalar", lambda a: a.activation(
                out=stt[:, 6:7], in_=stt[:, 3:4], func=AF.Ln, bias=epsc[:]),
                wait=[var_mk], inc=sA)
            wa = sA.mark()
            P.do("scalar", lambda a: a.activation(
                out=stt[:, 4:5], in_=stt[:, 6:7], func=AF.Exp, scale=-0.5),
                wait=[wa])
            rstd_mk = confirm(P, "scalar", sA)
            return (negm_mk, rstd_mk)

        def ln_back(P, stt, src, dst, fm, dst_free=None):
            negm_mk, rstd_mk = fm
            P.do("vector", lambda v: v.tensor_scalar(
                out=dst, in0=src, scalar1=stt[:, 5:6], scalar2=stt[:, 4:5],
                op0=ALU.add, op1=ALU.mult),
                wait=[negm_mk, rstd_mk, dst_free])
            return confirm(P, "vector", sV)

        def transpose_tile(P, srcT, t, dstbuf, psTb, psT_free, ready_mark,
                           extra_wait=None):
            last_pe = None
            for q in range(4):
                qi = (4 * t + q) % 2
                for u in range(4):
                    icc = 4 * q + u
                    P.do("tensor", lambda e, qi=qi, u=u, icc=icc:
                         e.matmul(psTb[qi][:, 128 * u:128 * u + 128],
                                  srcT[:, 128 * icc:128 * icc + 128],
                                  ident[:], start=True, stop=True),
                         wait=([ready_mark, psT_free[qi], extra_wait]
                               if u == 0 else []),
                         inc=(sPE if u == 3 else None))
                pe_mk = sPE.mark()
                dst = bass.AP(dstbuf, 2048 * q + 128 * t,
                              [[16 * 512, 128], [512, 4], [1, 128]])
                src3 = bass.AP(psTb[qi], 0, [[512, 128], [128, 4], [1, 128]])
                P.do("vector", lambda e, dst=dst, src3=src3: e.tensor_copy(
                    out=dst, in_=src3), wait=[pe_mk], inc=sVe)
                psT_free[qi] = sVe.mark()
                last_pe = pe_mk
            cmk = confirm(P, "vector", sVe)
            return cmk, last_pe

        P = Phase()
        P.do("sync", lambda e: e.dma_start(out=ident[:], in_=id_in),
             inc=sX, inck=16)
        for t in range(8):
            P.do("sync", lambda e, t=t: e.dma_start(
                out=masksb[:, 512 * t:512 * t + 512], in_=mk_in[t]),
                inc=sX, inck=16)
        for c in range(16):
            P.do("sync", lambda e, c=c: e.dma_start(
                out=scol[:, c:c + 1],
                in_=bass.AP(sc_in, 128 * c, [[1, 128], [1, 1]])),
                inc=sX, inck=16)
        P.do("vector", lambda e: e.memset(onesrow[:], 1.0))
        P.do("vector", lambda e: e.memset(negMc[:], NEG_SHIFT))
        P.do("vector", lambda e: e.memset(epsc[:], EPS))
        for g in range(32):
            P.do("gpsimd", lambda e, g=g: e.memset(Vt[g][:], 1.0))
        P.do("gpsimd", lambda e: e.engine_nop(), wait=[sX.mark()])
        P.run(nc, "init")

        with contextlib.ExitStack() as pst:
            WFST = [pst.enter_context(nc.sbuf_tensor(f"WFSp{i}", [128, D], F32))
                    for i in range(2)]
            BST = [pst.enter_context(nc.sbuf_tensor(f"BSTp{i}", [128, D], BF16))
                   for i in range(2)]
            QRB = pst.enter_context(nc.sbuf_tensor("QRB", [128, 4 * D], BF16))
            FST = [pst.enter_context(nc.sbuf_tensor(f"FSTp{i}", [128, 512], BF16))
                   for i in range(4)]
            dsc = pst.enter_context(nc.sbuf_tensor("dsc", [128, D], BF16))
            psT = [pst.enter_context(nc.psum_tensor(f"psTp{i}", [128, 512], F32))
                   for i in range(4)]
            WFS1 = [pst.enter_context(nc.sbuf_tensor(f"WFS1{i}", [128, 512], F32))
                    for i in range(2)]
            WKB = pst.enter_context(nc.sbuf_tensor("WKB", [128, 16 * 512], BF16))
            WVB = pst.enter_context(nc.sbuf_tensor("WVB", [128, 16 * 512], BF16))
            xt = [pst.enter_context(nc.sbuf_tensor(f"xt{i}", [128, D], F32))
                  for i in range(2)]
            PXF = pst.enter_context(nc.sbuf_tensor("PXF", [128, 16 * 512], BF16))
            psT1 = [pst.enter_context(nc.psum_tensor(f"psT1{i}", [128, 512], F32))
                    for i in range(2)]
            pkv = [pst.enter_context(nc.psum_tensor(f"pkv{i}", [128, 512], F32))
                   for i in range(2)]

            P = Phase()
            P.do("sync", lambda e: e.dma_start(out=WFST[0][0:1, 0:D], in_=bi_in),
                 inc=sW, inck=16)
            P.do("vector", lambda e: e.tensor_copy(out=browb[:],
                                                   in_=WFST[0][0:1, 0:D]),
                 wait=[sW.mark()], inc=sM)
            wfst_free_init = sM.mark()
            for c in range(16):
                P.do("gpsimd", lambda e, c=c: e.tensor_scalar(
                    out=dsc[:, 128 * c:128 * c + 128], in0=ident[:],
                    scalar1=scol[:, c:c + 1], scalar2=None, op0=ALU.mult))
            dsc_done = confirm(P, "gpsimd", sM)
            wfs_free = [None, None]
            for sl in range(32):
                i = sl % 2
                src = wk_in if sl < 16 else wv_in
                dstT = WKB if sl < 16 else WVB
                ic = sl % 16
                P.do("sync", lambda e, src=src, ic=ic, i=i: e.dma_start(
                    out=WFS1[i][:], in_=src[128 * ic:128 * ic + 128, :]),
                    wait=[wfs_free[i]], inc=sW, inck=16)
                P.do("vector", lambda e, dstT=dstT, ic=ic, i=i: e.tensor_copy(
                    out=dstT[:, 512 * ic:512 * ic + 512], in_=WFS1[i][:]),
                    wait=[sW.mark()], inc=sM)
                wfs_free[i] = sM.mark()
            wkv_cast = confirm(P, "vector", sM)

            wfst_free = [None, None]
            bst_free = [None, None]
            psTp_free = [None, None, None, None]
            qrb_free = [None]
            fst_free = [None, None, None, None]
            xt_free = [None, None]
            psT_free = [None, None]
            pkv_free = [None, None]
            xh_free = [None, None]
            pxf_free = None
            pend = None

            def emit_prep_slab(sl):
                kind = "wq" if sl < 16 else ("wo" if sl < 32 else "wf")
                ic = sl % 16
                src = {"wq": wq_in, "wo": wo_in, "wf": wf_in}[kind]
                i = sl % 2
                ib = sl % 2
                P.do("sync", lambda e, src=src, ic=ic, i=i: e.dma_start(
                    out=WFST[i][:], in_=src[128 * ic:128 * ic + 128, :]),
                    wait=[wfst_free[i] or wfst_free_init], inc=sW, inck=16)
                if kind != "wq":
                    ceng = "gpsimd" if kind == "wo" else "vector"
                    P.do(ceng, lambda e, i=i, ib=ib: e.tensor_copy(
                        out=BST[ib][:], in_=WFST[i][:]),
                        wait=[sW.mark(), bst_free[ib]])
                    wfst_free[i] = cast_mk = confirm(P, ceng, sM)
                if kind == "wq":
                    q4 = ic % 4
                    for p in range(16):
                        h1, _ = pair_heads(p)
                        srcp = bass.AP(WFST[i], 64 * h1,
                                       [[D, 128], [256, 2], [1, 64]])
                        P.do("gpsimd", lambda e, srcp=srcp, q4=q4, p=p:
                             e.tensor_copy(
                                 out=QRB[:, q4 * D + 128 * p:q4 * D + 128 * p + 128],
                                 in_=srcp),
                            wait=([sW.mark(), qrb_free[0]] if p == 0 else []))
                    wfst_free[i] = cast_mk = confirm(P, "gpsimd", sM)
                    if q4 == 3:
                        for p in range(16):
                            icq = ic - 3
                            dst = bass.AP(wqbf_h,
                                          (p * 128) * (NIC * 128) + icq * 128,
                                          [[NIC * 128, 128], [128, 4],
                                           [1, 128]])
                            srcq = bass.AP(QRB, 128 * p,
                                           [[4 * D, 128], [D, 4], [1, 128]])
                            eng = "scalar" if p % 2 == 0 else "sync"
                            P.do(eng, lambda e, dst=dst, srcq=srcq: e.dma_start(
                                out=dst, in_=srcq),
                                wait=[cast_mk], inc=sSC, inck=16)
                        qrb_free[0] = sSC.mark()
                elif kind == "wo":
                    P.do("scalar", lambda e, ic=ic, ib=ib: e.dma_start(
                        out=wobf[128 * ic:128 * ic + 128, :], in_=BST[ib][:]),
                        wait=[cast_mk], inc=sSC, inck=16)
                    bst_free[ib] = sSC.mark()
                else:
                    oc = ic
                    for q in range(4):
                        qi = q
                        for u in range(4):
                            icc = 4 * q + u
                            P.do("tensor", lambda e, ib=ib, qi=qi, u=u, icc=icc,
                                 oc=oc: e.matmul(
                                     psT[qi][:, 128 * u:128 * u + 128],
                                     BST[ib][:, 128 * icc:128 * icc + 128],
                                     dsc[:, 128 * oc:128 * oc + 128],
                                     start=True, stop=True),
                                 wait=([cast_mk, dsc_done, psTp_free[qi]]
                                       if u == 0 else []),
                                 inc=(sPE if u == 3 else None))
                        pe_mk = sPE.mark()
                        P.do("vector", lambda e, qi=qi: e.tensor_copy(
                            out=FST[qi][:], in_=psT[qi][:]),
                            wait=[pe_mk, fst_free[qi]])
                        psTp_free[qi] = confirm(P, "vector", sV)
                        dstw = bass.AP(wfbf_h, (512 * q) * D + 128 * oc,
                                       [[D, 128], [128 * D, 4], [1, 128]])
                        P.do("scalar", lambda e, qi=qi, dstw=dstw: e.dma_start(
                            out=dstw, in_=FST[qi][:]),
                            wait=[sV.mark()], inc=sSC, inck=16)
                        fst_free[qi] = sSC.mark()
                    bst_free[ib] = sPE.mark()

            def emit_kv_tile(kt):
                nonlocal pend, pxf_free
                if kt < 32:
                    i = kt % 2
                    P.do("sync", lambda e, kt=kt, i=i: e.dma_start(
                        out=xt[i][:], in_=xfull[128 * kt:128 * kt + 128, :]),
                        wait=[xt_free[i]], inc=sX, inck=16)
                    xmk = sX.mark()
                    fm = ln_front(P, stats[kt % 4], xt[i][:], [xmk])
                    nxt = (kt, i, kt % 2, fm)
                else:
                    nxt = None
                if pend is not None:
                    kt0, i0, ih0, fm0 = pend
                    sg, v = kt0 // 4, kt0 % 4
                    xh_mk = ln_back(P, stats[kt0 % 4], xt[i0][:], XH[ih0][:],
                                    fm0, dst_free=xh_free[ih0])
                    xt_free[i0] = xh_mk
                    _, last_pe = transpose_tile(P, XH[ih0], v, PXF, psT1,
                                                psT_free, xh_mk,
                                                extra_wait=pxf_free)
                    xh_free[ih0] = last_pe
                    if v == 3:
                        xf_ready = sVe.mark()
                        for g in range(8):
                            gi = g % 2
                            if g < 4:
                                kfc = g
                                for icx in range(NIC):
                                    P.do("tensor", lambda e, gi=gi, icx=icx,
                                         kfc=kfc: e.matmul(
                                             pkv[gi][:],
                                             WKB[:, 512 * icx + 128 * kfc:
                                                 512 * icx + 128 * kfc + 128],
                                             PXF[:, 512 * icx:512 * icx + 512],
                                             start=(icx == 0), stop=(icx == 15)),
                                         wait=([xf_ready, wkv_cast, pkv_free[gi]]
                                               if icx == 0 else []),
                                         inc=(sPE if icx == 15 else None))
                                pe_mk = sPE.mark()
                                P.do("vector", lambda e, gi=gi, kfc=kfc, sg=sg:
                                     e.tensor_copy(
                                         out=Kf[kfc][:, 512 * sg:512 * sg + 512],
                                         in_=pkv[gi][:]), wait=[pe_mk], inc=sVe)
                            else:
                                vtile = g - 4
                                for icx in range(NIC):
                                    P.do("tensor", lambda e, gi=gi, icx=icx,
                                         vt2=vtile: e.matmul(
                                             pkv[gi][:],
                                             PXF[:, 512 * icx + 128 * vt2:
                                                 512 * icx + 128 * vt2 + 128],
                                             WVB[:, 512 * icx:512 * icx + 512],
                                             start=(icx == 0), stop=(icx == 15)),
                                         wait=([pkv_free[gi]] if icx == 0 else []),
                                         inc=(sPE if icx == 15 else None))
                                pe_mk = sPE.mark()
                                dstv = bass.AP(Vt[4 * sg + vtile], 0,
                                               [[520, 128], [65, 8], [1, 64]])
                                srcp = bass.AP(pkv[gi], 0,
                                               [[512, 128], [64, 8], [1, 64]])
                                P.do("vector", lambda e, dstv=dstv, srcp=srcp:
                                     e.tensor_copy(out=dstv, in_=srcp),
                                     wait=[pe_mk], inc=sVe)
                            pkv_free[gi] = sVe.mark()
                        pxf_free = sPE.mark()
                pend = nxt

            for step in range(48):
                emit_prep_slab(step)
                if step < 33:
                    emit_kv_tile(step)
            prep_stores = sSC.mark()
            P.run(nc, "prepkv")


        with contextlib.ExitStack() as pst:
            XG = [pst.enter_context(nc.sbuf_tensor(f"XG{t}", [128, D], F32))
                  for t in range(4)]
            XF = pst.enter_context(nc.sbuf_tensor("XF", [128, 16 * 512], BF16))
            QP = pst.enter_context(nc.sbuf_tensor("QP", [128, 16 * 512], BF16))
            OF = pst.enter_context(nc.sbuf_tensor("OF", [128, 16 * 512], BF16))
            EB = [pst.enter_context(nc.sbuf_tensor(f"EB{i}", [128, 1024], BF16))
                  for i in range(2)]
            WST = [pst.enter_context(nc.sbuf_tensor(f"WST{i}", [128, D], BF16))
                   for i in range(4)]
            OSTG = [pst.enter_context(nc.sbuf_tensor(f"OSTG{i}", [65, 1024], F32))
                    for i in range(2)]
            RBa = [pst.enter_context(nc.sbuf_tensor(f"RBa{i}", [64, 512], F32))
                   for i in range(2)]
            RBb = [pst.enter_context(nc.sbuf_tensor(f"RBb{i}", [64, 512], F32))
                   for i in range(2)]
            RR1 = [pst.enter_context(nc.sbuf_tensor(f"RR1{i}", [64, 16], F32))
                   for i in range(2)]
            RR2 = [pst.enter_context(nc.sbuf_tensor(f"RR2{i}", [64, 16], F32))
                   for i in range(2)]

            xg_store = [None] * 4

            def emit_a(P, j, psTa, psQ, psT_free, psQ_free, wst_free, xh_free,
                       first_wait=None):
                pend = None
                for t in range(5):
                    if t < 4:
                        P.do("sync", lambda e, t=t, j=j: e.dma_start(
                            out=XG[t][:],
                            in_=xq[512 * j + 128 * t:512 * j + 128 * t + 128, :]),
                            wait=[xg_store[t]], inc=sX, inck=16)
                        xmk = sX.mark()
                        fm = ln_front(P, stats[t], XG[t][:], [xmk])
                        nxt = (t, fm)
                    else:
                        nxt = None
                    if pend is not None:
                        t0, fm0 = pend
                        xh_mk = ln_back(P, stats[t0], XG[t0][:], XH[t0 % 2][:],
                                        fm0, dst_free=xh_free[t0 % 2])
                        _, last_pe = transpose_tile(P, XH[t0 % 2], t0, XF,
                                                    psTa, psT_free, xh_mk)
                        xh_free[t0 % 2] = last_pe
                    pend = nxt
                xf_ready = sVe.mark()
                for p in range(16):
                    wi = p % 4
                    P.do("sync", lambda e, p=p, wi=wi: e.dma_start(
                        out=WST[wi][:], in_=wqbf[p]),
                        wait=[wst_free[wi],
                              (first_wait if p == 0 else None)],
                        inc=sW, inck=16)
                    wmk = sW.mark()
                    pi = p % 2
                    for icx in range(NIC):
                        P.do("tensor", lambda e, pi=pi, wi=wi, icx=icx:
                             e.matmul(psQ[pi][:],
                                      WST[wi][:, 128 * icx:128 * icx + 128],
                                      XF[:, 512 * icx:512 * icx + 512],
                                      start=(icx == 0), stop=(icx == 15)),
                             wait=([wmk, xf_ready, psQ_free[pi]]
                                   if icx == 0 else []),
                             inc=(sPE if icx == 15 else None))
                    wst_free[wi] = sPE.mark()
                    pe_mk = sPE.mark()
                    P.do("vector", lambda e, pi=pi, p=p: e.tensor_copy(
                        out=QP[:, 512 * p:512 * p + 512], in_=psQ[pi][:]),
                        wait=[pe_mk], inc=sVe)
                    psQ_free[pi] = sVe.mark()

            with contextlib.ExitStack() as ast:
                psTa = [ast.enter_context(
                    nc.psum_tensor("psTa0x%d" % i, [128, 512], F32))
                    for i in range(2)]
                psQ = [ast.enter_context(
                    nc.psum_tensor("psQ0x%d" % i, [128, 512], F32))
                    for i in range(2)]
                P = Phase()
                emit_a(P, 0, psTa, psQ, [None, None], [None, None],
                       [None] * 4, [None, None], first_wait=prep_stores)
                P.run(nc, "ga0")

            for j in range(4):
                NK = 8 * (j + 1)

                with contextlib.ExitStack() as ast:
                    psS = [ast.enter_context(
                        nc.psum_tensor(f"psS{j}{i}", [128, 1024], F32))
                        for i in range(2)]
                    psO = [ast.enter_context(
                        nc.psum_tensor(f"psO{j}{i}", [128, 512], F32))
                        for i in range(4)]
                    P = Phase()
                    expv, maskv, pvv, psofree = {}, {}, {}, {}
                    evacv, outb, outb2d, inb = {}, {}, {}, {}

                    def sched_pv(i2, p2, k2):
                        h1, h2 = pair_heads(p2)
                        kv1, kv2 = h1 // 4, h2 // 4
                        wl = []
                        if k2 == 0 and p2 >= 2:
                            wl.append(psofree[p2 - 2])
                        wl.append(maskv[i2] if k2 >= NK - 8 else expv[i2])
                        bi = i2 % 2
                        P.do("tensor", lambda e, p2=p2, k2=k2, kv1=kv1, bi=bi:
                             e.matmul(psO[p2 % 2][0:65, :],
                                      Vt[k2][:, 65 * kv1:65 * kv1 + 65],
                                      EB[bi][:, 0:512],
                                      start=(k2 == 0), stop=(k2 == NK - 1)),
                             wait=wl)
                        P.do("tensor", lambda e, p2=p2, k2=k2, kv2=kv2, bi=bi:
                             e.matmul(psO[2 + p2 % 2][0:65, :],
                                      Vt[k2][:, 65 * kv2:65 * kv2 + 65],
                                      EB[bi][:, 512:1024],
                                      start=(k2 == 0), stop=(k2 == NK - 1)),
                             inc=sPV)
                        pvv[i2] = sPV.mark()

                    pending_dve = []

                    def sched_epilogue(p2):
                        h1, h2 = pair_heads(p2)
                        pp = p2 % 2
                        base = pp * 1024
                        wl = [pvv[p2 * NK + NK - 1]]
                        if p2 >= 1:
                            wl.append(outb[p2 - 1])

                        def cl_copy(wl=wl, pp=pp, p2=p2):
                            wl2 = list(wl)
                            if p2 >= 2:
                                wl2.append(evacv[p2 - 2])
                            P.do("vector", lambda e: e.tensor_copy(
                                out=OSTG[pp][:, 0:512], in_=psO[pp][0:65, :]),
                                wait=wl2)
                            P.do("vector", lambda e: e.tensor_copy(
                                out=OSTG[pp][:, 512:1024],
                                in_=psO[2 + pp][0:65, :]))
                            psofree[p2] = confirm(P, "vector", sVr)
                        pending_dve.append(cl_copy)
                        svr1 = (sVr, sVr.v + 1)
                        P.do("sync", lambda e, base=base, pp=pp: e.dma_start(
                            out=bass.AP(rb_h, base, [[2048, 1], [1, 1024]]),
                            in_=OSTG[pp][64:65, :]),
                            wait=[svr1], inc=sRB, inck=16)
                        outb[p2] = sRB.mark()
                        P.do("sync", lambda e, pp=pp, base=base: e.dma_start(
                            out=RR1[pp][:],
                            in_=bass.AP(rb_h, base, [[16, 64], [1, 16]])),
                            wait=[outb[p2]], inc=sRB, inck=16)
                        rr1_mk = sRB.mark()

                        def cl_recip(rr1_mk=rr1_mk, pp=pp, p2=p2):
                            P.do("vector", lambda e: e.reciprocal(
                                out=RR2[pp][:], in_=RR1[pp][:]),
                                wait=[rr1_mk,
                                      (outb2d.get(p2 - 2) if p2 >= 2 else None)])
                            confirm(P, "vector", sVr)
                        pending_dve.append(cl_recip)
                        svr2 = (sVr, sVr.v + 2)
                        P.do("sync", lambda e, pp=pp, base=base: e.dma_start(
                            out=bass.AP(rb_h, base + 1024, [[16, 64], [1, 16]]),
                            in_=RR2[pp][:]),
                            wait=[svr2], inc=sRB, inck=16)
                        outb2d[p2] = sRB.mark()
                        P.do("sync", lambda e, pp=pp, base=base: e.dma_start(
                            out=RBa[pp][:],
                            in_=bass.AP(rb_h, base + 1024, [[0, 64], [1, 512]])),
                            wait=[outb2d[p2],
                                  (evacv[p2 - 2] if p2 >= 2 else None)],
                            inc=sRB, inck=16)
                        P.do("sync", lambda e, pp=pp, base=base: e.dma_start(
                            out=RBb[pp][:],
                            in_=bass.AP(rb_h, base + 1024 + 512,
                                        [[0, 64], [1, 512]])),
                            inc=sRB, inck=16)
                        inb[p2] = sRB.mark()
                        inb_mk = inb[p2]
                        for hh, col, rbuf in ((h1, 0, RBa[pp]),
                                              (h2, 512, RBb[pp])):
                            dst = OF[64 * (hh % 2):64 * (hh % 2) + 64,
                                     512 * (hh // 2):512 * (hh // 2) + 512]
                            P.do("gpsimd",
                                 lambda e, col=col, rbuf=rbuf, dst=dst, pp=pp:
                                 e.tensor_tensor(
                                     out=dst,
                                     in0=OSTG[pp][0:64, col:col + 512],
                                     in1=rbuf[:], op=ALU.mult),
                                 wait=[inb_mk], inc=(sVo if hh == h2 else None))
                        evacv[p2] = sVo.mark()

                    iters = [(p, k) for p in range(16) for k in range(NK)]
                    for i, (p, k) in enumerate(iters):
                        c1 = p // 4
                        bi = i % 2
                        wl = [expv[i - 2]] if i >= 2 else []
                        P.do("tensor", lambda e, c1=c1, k=k, p=p, bi=bi:
                             e.matmul(psS[bi][:, 0:512],
                                      Kf[c1][0:64, 128 * k:128 * k + 128],
                                      QP[0:64, 512 * p:512 * p + 512],
                                      start=True, stop=True), wait=wl)
                        P.do("tensor", lambda e, c1=c1, k=k, p=p, bi=bi:
                             e.matmul(psS[bi][:, 512:1024],
                                      Kf[c1][64:128, 128 * k:128 * k + 128],
                                      QP[64:128, 512 * p:512 * p + 512],
                                      start=True, stop=True), inc=sS)
                        smk = sS.mark()
                        wl = [smk]
                        if i >= 2:
                            wl.append(pvv[i - 2])
                        P.do("scalar", lambda e, bi=bi: e.activation(
                            out=EB[bi][:], in_=psS[bi][:], func=AF.Exp,
                            bias=negMc[:], scale=0.125), wait=wl)
                        expv[i] = confirm(P, "scalar", sE)
                        if k >= NK - 8:
                            tix = k - (NK - 8)
                            msrc = bass.AP(masksb, 512 * tix,
                                           [[4096, 128], [0, 2], [1, 512]])
                            P.do("vector", lambda e, bi=bi, msrc=msrc: e.tensor_mul(
                                out=EB[bi][:], in0=EB[bi][:], in1=msrc),
                                wait=[expv[i]])
                            maskv[i] = confirm(P, "vector", sM)
                        if pending_dve:
                            pending_dve.pop(0)()
                        if i >= 1:
                            p2, k2 = iters[i - 1]
                            sched_pv(i - 1, p2, k2)
                            if k2 == NK - 1:
                                sched_epilogue(p2)
                    sched_pv(len(iters) - 1, 15, NK - 1)
                    sched_epilogue(15)
                    while pending_dve:
                        pending_dve.pop(0)()
                    P.run(nc, f"gb{j}")

                with contextlib.ExitStack() as ast:
                    psA = [ast.enter_context(
                        nc.psum_tensor(f"psA{j}{i}", [128, 512], F32))
                        for i in range(8)]
                    P = Phase()
                    wst_free = [None] * 4
                    psA_free = [None] * 8
                    def proj_stream(weight_ap, lhs_buf, bias_add, ready_marks,
                                    first_load_wait=None):
                        tile_done = [None] * 4
                        for och in range(2):
                            for afc in range(16):
                                wi = afc % 4
                                P.do("sync", lambda e, afc=afc, och=och, wi=wi,
                                     w=weight_ap: e.dma_start(
                                         out=WST[wi][:, 0:1024],
                                         in_=w[128 * afc:128 * afc + 128,
                                               1024 * och:1024 * och + 1024]),
                                     wait=[wst_free[wi],
                                           (first_load_wait
                                            if (och == 0 and afc == 0) else None)],
                                     inc=sW, inck=16)
                                wmk = sW.mark()
                                stop = (afc == 15) and not bias_add
                                for bidx in range(8):
                                    t, o2 = bidx // 2, bidx % 2
                                    if afc == 0:
                                        wl = [wmk, psA_free[bidx]] + ready_marks
                                    else:
                                        wl = [wmk] if bidx == 0 else []
                                    P.do("tensor", lambda e, afc=afc, t=t, o2=o2,
                                         wi=wi, bidx=bidx, stop=stop:
                                         e.matmul(psA[bidx][:],
                                                  lhs_buf[:, 512 * afc + 128 * t:
                                                          512 * afc + 128 * t + 128],
                                                  WST[wi][:, 512 * o2:512 * o2 + 512],
                                                  start=(afc == 0), stop=stop),
                                         wait=wl,
                                         inc=(sPE if bidx == 7 else None))
                                wst_free[wi] = sPE.mark()
                            if bias_add:
                                for bidx in range(8):
                                    t, o2 = bidx // 2, bidx % 2
                                    col = 1024 * och + 512 * o2
                                    P.do("tensor", lambda e, bidx=bidx, col=col:
                                         e.matmul(psA[bidx][:],
                                                  onesrow[0:1, :],
                                                  browb[0:1, col:col + 512],
                                                  start=False, stop=True),
                                         inc=(sPE if bidx == 7 else None))
                            pe_mk = sPE.mark()
                            for bidx in range(8):
                                t, o2 = bidx // 2, bidx % 2
                                col = 1024 * och + 512 * o2
                                P.do("vector", lambda e, bidx=bidx, t=t, col=col:
                                     e.tensor_add(
                                         out=XG[t][:, col:col + 512],
                                         in0=psA[bidx][:],
                                         in1=XG[t][:, col:col + 512]),
                                     wait=[pe_mk], inc=sVe)
                                psA_free[bidx] = sVe.mark()
                                if och == 1 and bidx % 2 == 1:
                                    tile_done[t] = confirm(P, "vector", sVe)
                        return tile_done

                    of_ready = sVo.mark()
                    x2_done = proj_stream(
                        wobf, OF, False, [of_ready],
                        first_load_wait=(prep_stores if j == 0 else None))

                    psT_free2 = [psA_free[0], psA_free[1]]
                    xh_free = [None, None]
                    pend = None
                    for t in range(5):
                        if t < 4:
                            fm = ln_front(P, stats[t], XG[t][:], [x2_done[t]])
                            nxt = (t, fm)
                        else:
                            nxt = None
                        if pend is not None:
                            t0, fm0 = pend
                            xh_mk = ln_back(P, stats[t0], XG[t0][:], XH[t0 % 2][:],
                                            fm0, dst_free=xh_free[t0 % 2])
                            _, last_pe = transpose_tile(P, XH[t0 % 2], t0, XF,
                                                        [psA[0], psA[1]],
                                                        psT_free2, xh_mk)
                            xh_free[t0 % 2] = last_pe
                        pend = nxt
                    xf2_ready = sVe.mark()
                    psA_free[0] = psT_free2[0]
                    psA_free[1] = psT_free2[1]

                    y_done = proj_stream(wfbf, XF, True, [xf2_ready])

                    for t in range(4):
                        P.do("sync", lambda e, t=t, j=j: e.dma_start(
                            out=out_d[512 * j + 128 * t:512 * j + 128 * t + 128, :],
                            in_=XG[t][:]), wait=[y_done[t]], inc=sOUT, inck=16)
                        xg_store[t] = sOUT.mark()
                    if j < 3:
                        emit_a(P, j + 1, [psA[2], psA[3]], [psA[4], psA[5]],
                               [psA_free[2], psA_free[3]],
                               [psA_free[4], psA_free[5]], wst_free, xh_free)
                    P.run(nc, f"gcd{j}")


            P = Phase()
            P.do("gpsimd", lambda e: e.engine_nop(), wait=[sOUT.mark()])
            P.run(nc, "fin")

    return nc


def _make_masks(par):
    m = np.zeros((8, 128, 512), dtype=np.float32)
    ii = np.arange(128)[:, None]
    jj = np.arange(512)[None, :]
    for t in range(8):
        if par == 0:
            if t < 4:
                m[t] = (128 * t + ii <= jj).astype(np.float32)
        else:
            if t < 4:
                m[t] = 1.0
            else:
                m[t] = (128 * (t - 4) + ii <= jj).astype(np.float32)
    return m.astype(ml_dtypes.bfloat16)


def kernel(L0_attn_x, L0_attn_Wq, L0_attn_Wk, L0_attn_Wv, L0_attn_Wo,
           L0_ffn_weight, L0_ffn_scale_weight, L0_ffn_bias):
    x = np.ascontiguousarray(np.asarray(L0_attn_x, dtype=np.float32))
    wq = np.ascontiguousarray(np.asarray(L0_attn_Wq, dtype=np.float32))
    wk = np.ascontiguousarray(np.asarray(L0_attn_Wk, dtype=np.float32))
    wv = np.ascontiguousarray(np.asarray(L0_attn_Wv, dtype=np.float32))
    wo = np.ascontiguousarray(np.asarray(L0_attn_Wo, dtype=np.float32))
    wf = np.ascontiguousarray(np.asarray(L0_ffn_weight, dtype=np.float32))
    sc = np.ascontiguousarray(np.asarray(L0_ffn_scale_weight, dtype=np.float32))
    bi = np.ascontiguousarray(np.asarray(L0_ffn_bias, dtype=np.float32))
    ident = np.eye(128, dtype=ml_dtypes.bfloat16)
    masks = [_make_masks(0), _make_masks(1)]

    nc = build_program()
    in_maps = []
    for c in range(8):
        b, par = c // 2, c % 2
        xq_c = np.ascontiguousarray(np.concatenate(
            [x[b, 1024 * j + 512 * par:1024 * j + 512 * par + 512]
             for j in range(4)], axis=0))
        in_maps.append({
            "xfull": np.ascontiguousarray(x[b]),
            "xq": xq_c,
            "wq": wq, "wk": wk, "wv": wv, "wo": wo,
            "wffn": wf, "scale": sc, "bias": bi,
            "masks": masks[par], "ident": ident,
        })
    res = run_bass_kernel_spmd(nc, in_maps, core_ids=list(range(8)))

    out = np.empty((B, S, D), dtype=np.float32)
    for c in range(8):
        b, par = c // 2, c % 2
        oc = res.results[c]["out"]
        for j in range(4):
            out[b, 1024 * j + 512 * par:1024 * j + 512 * par + 512] = \
                oc[512 * j:512 * j + 512]
    return out


if __name__ == "__main__":
    prog = build_program()
    print("build ok, instructions:", prog.n_instructions())
